# version 19
# baseline (speedup 1.0000x reference)
import contextlib
import os
import numpy as np
import ml_dtypes
import concourse.bass as bass
import concourse.mybir as mybir
from concourse.bass_utils import run_bass_kernel_spmd

F32 = mybir.dt.float32
BF16 = mybir.dt.bfloat16
I32 = mybir.dt.int32
AF = mybir.ActivationFunctionType
ALU = mybir.AluOpType
AX = mybir.AxisListType.X

P = 128
TA = 2048
TO = 1024
C = 2048
KC = 512
MLP = 16384
NSL = 2048
NSUB = 96
NPUMP = 101
KLV = int(os.environ.get("KLV", "9"))
KCUT = int(os.environ.get("KCUT", "9"))
KDBG = os.environ.get("KDBG", "0") == "1"


class Sem:
    def __init__(self, h, name):
        self.h, self.name = h, name


class MS:
    def __init__(self, prev=None, semof_prev=None):
        self.prev = prev if prev is not None else {}
        self.semof_prev = semof_prev if semof_prev is not None else {}
        self.out = {}
        self.cnt = {}
        self.semof = {}

    def inc(self, inst, sem, k=1, name=None):
        inst.then_inc(sem.h, k)
        self.cnt[sem.name] = self.cnt.get(sem.name, 0) + k
        if name is not None:
            assert name not in self.out, name
            self.out[name] = self.cnt[sem.name]
            self.semof[name] = sem.name
        return self.cnt[sem.name]

    def mark(self, sem, name):
        assert name not in self.out, name
        self.out[name] = self.cnt.get(sem.name, 0)
        self.semof[name] = sem.name

    def wait(self, eng, sem, name):
        so = self.semof_prev.get(name)
        assert so is None or so == sem.name, (name, so, sem.name)
        v = self.prev.get(name, 0)
        if v > 0:
            eng.wait_ge(sem.h, v)


def build(ms):
    nc = bass.Bass("TRN2", target_bir_lowering=False, debug=False)
    xo = nc.dram_tensor("xo", [TO, C], F32, kind="ExternalInput").ap()
    xz = nc.dram_tensor("xz", [TO, C], F32, kind="ExternalInput").ap()
    wq = nc.dram_tensor("wq", [C, C], F32, kind="ExternalInput").ap()
    wk = nc.dram_tensor("wk", [C, KC], F32, kind="ExternalInput").ap()
    wv = nc.dram_tensor("wv", [C, KC], F32, kind="ExternalInput").ap()
    wo = nc.dram_tensor("wo", [C, C], F32, kind="ExternalInput").ap()
    pks = {"g": nc.dram_tensor("gpk", [NSL, 1024], I32, kind="ExternalInput").ap(),
           "u": nc.dram_tensor("upk", [NSL, 1024], I32, kind="ExternalInput").ap(),
           "d": nc.dram_tensor("dpk", [C, 1024], I32, kind="ExternalInput").ap()}
    scs = {"g": nc.dram_tensor("gsc", [NSL, 64], F32, kind="ExternalInput").ap(),
           "u": nc.dram_tensor("usc", [NSL, 64], F32, kind="ExternalInput").ap(),
           "d": nc.dram_tensor("dsc", [C, 64], F32, kind="ExternalInput").ap()}
    identd = nc.dram_tensor("identd", [P, P], BF16, kind="ExternalInput").ap()
    onescd = nc.dram_tensor("onescd", [P, 1], BF16, kind="ExternalInput").ap()
    onesrfd = nc.dram_tensor("onesrfd", [1, P], F32, kind="ExternalInput").ap()
    tblq = nc.dram_tensor("tblq", [P, 2 * TO], BF16, kind="ExternalInput").ap()
    epsd = nc.dram_tensor("epsd", [P, 1], F32, kind="ExternalInput").ap()
    tblk = nc.dram_tensor("tblk", [P, 2 * TA], BF16, kind="ExternalInput").ap()
    out = nc.dram_tensor("out", [TO, C], F32, kind="ExternalOutput").ap()
    if KDBG:
        d_q = nc.dram_tensor("d_q", [16 * P, TO], BF16, kind="ExternalOutput").ap()
        d_k = nc.dram_tensor("d_k", [P, 8192], BF16, kind="ExternalOutput").ap()
        d_v = nc.dram_tensor("d_v", [P, 8192], BF16, kind="ExternalOutput").ap()
        d_qp = nc.dram_tensor("d_qp", [16 * P, TO], BF16, kind="ExternalOutput").ap()
        d_enc = nc.dram_tensor("d_enc", [P, 16384], BF16, kind="ExternalOutput").ap()
        d_h = nc.dram_tensor("d_h", [P, 16384], BF16, kind="ExternalOutput").ap()
        d_eb = nc.dram_tensor("d_eb", [P, 8192], BF16, kind="ExternalOutput").ap()
        d_wg = nc.dram_tensor("d_wg", [128, P * 2048], BF16, kind="ExternalOutput").ap()
        d_wd = nc.dram_tensor("d_wd", [MLP, C], BF16, kind="ExternalOutput").ap()

    q_sp = nc.dram_tensor("q_sp", [16 * P, TO], BF16, kind="Internal")
    wo_bf = nc.dram_tensor("wo_bf", [P, 32768], BF16, kind="Internal")
    agi = {"g": nc.dram_tensor("agi_g", [16, P * 2048], BF16, kind="Internal"),
           "u": nc.dram_tensor("agi_u", [16, P * 2048], BF16, kind="Internal")}
    ago = {"g": nc.dram_tensor("ago_g", [128, P * 2048], BF16, kind="Internal"),
           "u": nc.dram_tensor("ago_u", [128, P * 2048], BF16, kind="Internal")}
    agi_dh = [nc.dram_tensor(f"agi_d{h}", [NSL, 1024], BF16, kind="Internal")
              for h in range(2)]
    ago_dh = [nc.dram_tensor(f"ago_d{h}", [MLP, 1024], BF16, kind="Internal")
              for h in range(2)]

    st = contextlib.ExitStack()
    e = st.enter_context
    with st:
        s_id = e(nc.sbuf_tensor("s_id", [P, P], BF16))
        s_oc = e(nc.sbuf_tensor("s_oc", [P, 1], BF16))
        s_orf = e(nc.sbuf_tensor("s_orf", [1, P], F32))
        EPSB = e(nc.sbuf_tensor("EPSB", [P, 1], F32))
        A2 = e(nc.sbuf_tensor("A2", [P, 16384], BF16))
        A3 = e(nc.sbuf_tensor("A3", [P, 16384], BF16))
        A4 = e(nc.sbuf_tensor("A4", [P, 8192], BF16))
        A5 = e(nc.sbuf_tensor("A5", [P, 8192], BF16))
        A6 = e(nc.sbuf_tensor("A6", [P, 8192], BF16))
        A7 = e(nc.sbuf_tensor("A7", [P, 8192], BF16))
        WQB = e(nc.sbuf_tensor("WQB", [P, 4096], BF16))
        QEV = e(nc.sbuf_tensor("QEV", [P, 2048], BF16))
        QBC = e(nc.sbuf_tensor("QBC", [P, 1024], BF16))
        XT = e(nc.sbuf_tensor("XT", [P, 4096], F32))
        SQB = e(nc.sbuf_tensor("SQB", [P, 2048], BF16))
        H2 = e(nc.sbuf_tensor("H2", [P, 2048], BF16))
        RT = e(nc.sbuf_tensor("RT", [P, 2048], BF16))
        STA = e(nc.sbuf_tensor("STA", [P, 16], F32))
        ROWS = e(nc.sbuf_tensor("ROWS", [1, 3072], F32))
        RBS = e(nc.sbuf_tensor("RBS", [P, 1024], F32))
        DPK = e(nc.sbuf_tensor("DPK", [P, 1024], I32))
        DTS = e(nc.sbuf_tensor("DTS", [P, 2048], BF16))
        DS4 = e(nc.sbuf_tensor("DS4", [P, 128], F32))
        DSX = e(nc.sbuf_tensor("DSX", [P, 1024], BF16))
        DW = e(nc.sbuf_tensor("DW", [P, 2048], BF16))
        DWT = e(nc.sbuf_tensor("DWT", [P, 2048], BF16))

        def mksem(nm):
            return Sem(e(nc.semaphore(nm)), nm)

        def mksem2(nm):
            return [mksem(nm + "0"), mksem(nm + "1")]

        sV = mksem("sV")
        sA = mksem("sA")
        sT = mksem("sT")
        sAG = mksem("sAG")
        sCONST = mksem("sCONST")
        sWKV = mksem("sWKV")
        sWQ = mksem2("sWQ")
        sXL = mksem2("sXL")
        sQS = mksem2("sQS")
        sKSW = mksem("sKSW")
        sQH = mksem2("sQH")
        sPK = mksem2("sPK")
        sSC = mksem2("sSC")
        sDW = mksem2("sDW")
        sDWA = mksem2("sDWA")
        sXC = mksem("sXC")
        sWO = mksem2("sWO")
        sO1 = mksem2("sO1")
        sXR = mksem2("sXR")
        sWG = mksem2("sWG")
        sWD = mksem2("sWD")
        sAC = mksem2("sAC")

        kfm = A5
        vtm = A7
        EB = A6
        A4I = A4[:].bitcast(I32)
        DCC = A4I[:, 0:1024]
        DB1 = A4I[:, 1024:2048]
        DB2 = A4I[:, 2048:3072]
        DB3 = A4I[:, 3072:4096]

        def lnx(j):
            return A3[:, (j % 2) * 8192:(j % 2) * 8192 + 8192]

        cosq = A2[:, 0:1024]
        sinq = A2[:, 1024:2048]

        def qevsl(hd):
            k = hd % 3
            return QEV[:, k * 1024:k * 1024 + 1024] if k < 2 \
                else A2[:, 2048:3072]

        def rtsl(hd):
            k = hd % 3
            return RT[:, k * 1024:k * 1024 + 1024] if k < 2 \
                else A2[:, 3072:4096]
        cosk = A2[:, 4096:6144]
        sink = A2[:, 6144:8192]

        def dsub(i):
            return "gud"[i // 32], (i % 32) // 2, (i % 32) % 2

        def dec_load(g, i):
            if i < 0 or i >= NSUB:
                return
            m, rt, hf = dsub(i)
            ms.wait(g, sV, f"dv_{i-2}")
            sl = (i % 2) * 512
            inst = g.dma_start(out=DPK[:, sl:sl + 512],
                               in_=pks[m][rt * P:(rt + 1) * P, hf * 512:(hf + 1) * 512])
            ms.inc(inst, sPK[i % 2], 16, name=f"dl_{i}")
            if hf == 0:
                ssl = (rt % 2) * 64
                ms.wait(g, sV, f"dv_{i-3}")
                inst = g.dma_start(out=DS4[:, ssl:ssl + 64],
                                   in_=scs[m][rt * P:(rt + 1) * P, :])
                ms.inc(inst, sSC[rt % 2], 16, name=f"scl_{i}")

        def dec_v(v, i):
            if i < 0 or i >= NSUB:
                return
            m, rt, hf = dsub(i)
            ms.wait(v, sPK[i % 2], f"dl_{i}")
            ms.wait(v, sSC[rt % 2], f"scl_{i - hf}")
            pk = DPK[:, (i % 2) * 512:(i % 2) * 512 + 512]
            Cc = DCC
            v.tensor_single_scalar(DCC[:, 0:1024:2], pk, 15, ALU.bitwise_and)
            v.tensor_single_scalar(DCC[:, 1:1024:2], pk, 4, ALU.logical_shift_right)
            v.tensor_single_scalar(DB1, Cc, 7, ALU.bitwise_and)
            v.tensor_single_scalar(DB2, DB1, 1, ALU.logical_shift_right)
            v.tensor_scalar(DB3, Cc, 1, 2, op0=ALU.bitwise_and, op1=ALU.bitwise_or)
            v.tensor_tensor(DB3, DB3, DB2, ALU.logical_shift_left)
            v.tensor_scalar(DB1, DB1, 2, -2, op0=ALU.min, op1=ALU.add)
            v.tensor_tensor(DB3, DB3, DB1, ALU.add)
            v.tensor_single_scalar(DB2, Cc, 3, ALU.logical_shift_right)
            v.tensor_scalar(DB2, DB2, -2, 1, op0=ALU.mult, op1=ALU.add)
            tss = DTS[:, (i % 2) * 1024:(i % 2) * 1024 + 1024]
            v.tensor_tensor(tss, DB3, DB2, ALU.mult)
            ssl = (rt % 2) * 64
            sc4 = DS4[:, ssl + hf * 32: ssl + hf * 32 + 32]
            v.tensor_single_scalar(DSX[:].rearrange("p (a b) -> p a b", b=32),
                                   sc4.unsqueeze(2).to_broadcast((P, 32, 32)),
                                   0.25, ALU.mult)
            ms.wait(v, sT, f"dt_{i-2}")
            dws = DW[:, (i % 2) * 1024:(i % 2) * 1024 + 1024]
            inst = v.tensor_tensor(dws, tss, DSX[:], ALU.mult)
            ms.inc(inst, sV, 1, name=f"dv_{i}")

        def dec_t(t, pdec, i):
            if i < 0 or i >= NSUB:
                return
            ms.wait(t, sV, f"dv_{i}")
            ms.wait(t, sV, f"de_{i-2}")
            dws = DW[:, (i % 2) * 1024:(i % 2) * 1024 + 1024]
            pb = pdec[i % 2]
            last = None
            for j in range(8):
                last = t.transpose(pb[:, j * P:(j + 1) * P], dws[:, j * P:(j + 1) * P],
                                   s_id[:])
            ms.inc(last, sT, 1, name=f"dt_{i}")

        def dec_e(v, pdec, i):
            if i < 0 or i >= NSUB:
                return
            ms.wait(v, sT, f"dt_{i}")
            if i >= 2:
                ms.wait(v, sDW[i % 2], f"dw_{i-2}")
            inst = v.tensor_copy(DWT[:, (i % 2) * 1024:(i % 2) * 1024 + 1024],
                                 pdec[i % 2][:])
            ms.inc(inst, sV, 1, name=f"de_{i}")

        def dec_w(g, i):
            if i < 0 or i >= NSUB:
                return
            m, rt, hf = dsub(i)
            ms.wait(g, sV, f"de_{i}")
            src = DWT[:, (i % 2) * 1024:(i % 2) * 1024 + 1024]
            if m in ("g", "u"):
                dst = agi[m].ap().rearrange("a (p f) -> a p f", p=P)[
                    rt, :, hf * 1024:hf * 1024 + 1024]
                ms.inc(g.dma_start(out=dst, in_=src), sDW[i % 2], 16,
                       name=f"dw_{i}")
            else:
                adh = agi_dh[rt // 8].ap()
                for j in range(8):
                    piece = src[:, j * P:(j + 1) * P]
                    dst = adh[hf * 1024 + j * P:hf * 1024 + (j + 1) * P,
                              (rt % 8) * P:(rt % 8 + 1) * P]
                    nm = f"dw_{i}" if j == 7 else None
                    ms.inc(g.dma_start(out=dst, in_=piece), sDW[i % 2], 16,
                           name=nm)

        def ag_trig(g, m2):
            iN = {"g": 31, "u": 63, "d1": 79, "d2": 95}[m2]
            ms.wait(g, sDW[iN % 2], f"dw_{iN}")
            if m2 in ("d1", "d2"):
                src_, dst_ = agi_dh[0 if m2 == "d1" else 1], \
                    ago_dh[0 if m2 == "d1" else 1]
            else:
                src_, dst_ = agi[m2], ago[m2]
            inst = g.collective_compute(
                "AllGather", ALU.bypass,
                replica_groups=[[0, 1, 2, 3, 4, 5, 6, 7]],
                ins=[src_.ap().opt()], outs=[dst_.ap().opt()])
            ms.inc(inst, sAG, 1, name=f"ag_{m2}")

        def pump_g(g, p):
            dec_load(g, p)
            dec_w(g, p - 5)
            if p - 5 == 31:
                ag_trig(g, "g")

        def pump_v(v, pdec, p):
            dec_v(v, p - 2)
            dec_e(v, pdec, p - 4)

        def pump_t(t, pdec, p):
            dec_t(t, pdec, p - 3)

        with nc.Block() as blk:
            @blk.sync
            def _(s):
                for src, dst in ((identd, s_id[:]), (onescd, s_oc[:]),
                                 (onesrfd, s_orf[:]), (epsd, EPSB[:])):
                    ms.inc(s.dma_start(out=dst, in_=src), sCONST, 16)
                ms.inc(s.dma_start(out=A2[:, 0:2048], in_=tblq), sCONST, 16)
                ms.inc(s.dma_start(out=A2[:, 4096:8192], in_=tblk), sCONST, 16)
                ms.mark(sCONST, "const")

            @blk.gpsimd
            def _(g):
                src = wk.rearrange("(ch p) n -> p ch n", p=P)
                ms.inc(g.dma_start(out=A4[:].rearrange("p (a b) -> p a b", b=KC),
                                   in_=src), sWKV, 16)
                src = wv.rearrange("(ch p) n -> p ch n", p=P)
                ms.inc(g.dma_start(out=A6[:].rearrange("p (a b) -> p a b", b=KC),
                                   in_=src), sWKV, 16)
                ms.mark(sWKV, "wkv")

        ps1 = contextlib.ExitStack()
        pT = [ps1.enter_context(nc.psum_tensor(f"pT{i}", [P, 1024], BF16))
              for i in range(2)]
        pQ = [ps1.enter_context(nc.psum_tensor(f"pQ{i}", [P, 512], F32))
              for i in range(2)]
        pKV = [ps1.enter_context(nc.psum_tensor(f"pKV{i}", [P, 512], F32))
               for i in range(2)]
        pR = [ps1.enter_context(nc.psum_tensor(f"pR{i}", [1, 512], F32))
              for i in range(2)]

        def vsync(v, inst):
            c = ms.inc(inst, sV, 1)
            v.wait_ge(sV.h, c)

        def asyncf(a, inst):
            c = ms.inc(inst, sA, 1)
            a.wait_ge(sA.h, c)

        def ln_stats_v(v, xs, tag):
            vsync(v, v.reduce_sum(STA[:, 0:1], xs, axis=AX))
            vsync(v, v.tensor_single_scalar(STA[:, 1:2], STA[:, 0:1], 1.0 / C,
                                            ALU.mult))
            vsync(v, v.tensor_tensor(STA[:, 8:9], STA[:, 1:2], STA[:, 1:2], ALU.mult))
            inst = v.tensor_scalar(STA[:, 2:3], STA[:, 8:9], -1.0, 1e-5,
                                   op0=ALU.mult, op1=ALU.add)
            ms.inc(inst, sV, 1, name=f"lm_{tag}")

        def ln_stats_a(a, xs, tag):
            ms.wait(a, sV, f"lm_{tag}")
            asyncf(a, a.activation(SQB[:], xs, AF.Square, accum_out=STA[:, 3:4]))
            asyncf(a, a.activation(STA[:, 7:8], STA[:, 3:4], AF.Ln, bias=STA[:, 2:3],
                                   scale=1.0 / C))
            inst = a.activation(STA[:, 5:6], STA[:, 7:8], AF.Exp, scale=-0.5)
            ms.inc(inst, sA, 1, name=f"lr_{tag}")

        def lnx_evac(v, i):
            j = i // 4
            tt = i % 4
            ms.wait(v, sT, f"lt_{i}")
            dst = lnx(j).rearrange("p (a b) -> p a b", b=512)
            last = None
            for gi in range(2):
                src = pT[gi][:].rearrange("p (a b) -> p a b", b=P)
                d3 = dst[:, gi * 8:gi * 8 + 8, tt * P:tt * P + P]
                last = v.tensor_copy(d3, src)
            ms.inc(last, sV, 1, name=f"le_{i}")

        with nc.Block() as blk:
            @blk.gpsimd
            def _(g):
                for qc in range(16 if KCUT >= 3 else 0):
                    if qc >= 2:
                        ms.wait(g, sT, f"qmm_{qc-2}_1")
                    srcw = wq[:, qc * P:(qc + 1) * P].rearrange("(ch p) n -> p ch n",
                                                                p=P)
                    dst = WQB[:, (qc % 2) * 2048:(qc % 2) * 2048 + 2048]
                    inst = g.dma_start(out=dst.rearrange("p (a b) -> p a b", b=P),
                                       in_=srcw)
                    ms.inc(inst, sWQ[qc % 2], 16, name=f"wq_{qc}")

            @blk.sync
            def _(s):
                def xld(i):
                    ms.wait(s, sV, f"lh_x{i-2}")
                    src = (xo if i < 8 else xz)[(i % 8) * P:(i % 8) * P + P, :]
                    inst = s.dma_start(out=XT[:, (i % 2) * C:(i % 2) * C + C], in_=src)
                    ms.inc(inst, sXL[i % 2], 16, name=f"xld_{i}")

                for i in range(10):
                    xld(i)
                for qc in range(16 if KCUT >= 3 else 0):
                    ms.wait(s, sV, f"qev_{qc}_1")
                    src = QEV[:, (qc % 2) * 1024:(qc % 2) * 1024 + 1024]
                    inst = s.dma_start(out=q_sp.ap()[qc * P:(qc + 1) * P, :], in_=src)
                    ms.inc(inst, sQS[qc % 2], 16, name=f"qsp_{qc}")
                for i in range(10, 16):
                    xld(i)
                for kc in range(4 if KCUT >= 4 else 0):
                    ms.wait(s, sV, f"krn_{kc}")
                    ms.inc(s.dma_start(out=SQB[0:64, :], in_=RT[64:128, 0:2048]), sKSW, 16)
                    inst = s.dma_start(out=SQB[64:128, :], in_=RT[0:64, 0:2048])
                    ms.inc(inst, sKSW, 16, name=f"ksw_{kc}")

            @blk.vector
            def _(v):
                ms.wait(v, sCONST, "const")

                def ln_tile(i):
                    xs = XT[:, (i % 2) * C:(i % 2) * C + C]
                    ms.wait(v, sXL[i % 2], f"xld_{i}")
                    ln_stats_v(v, xs, f"x{i}")
                    ms.wait(v, sA, f"lr_x{i}")
                    ms.wait(v, sT, f"lt_{i-1}")
                    inst = v.tensor_scalar(H2[:], xs, STA[:, 1:2], STA[:, 5:6],
                                           op0=ALU.subtract, op1=ALU.mult)
                    ms.inc(inst, sV, 1, name=f"lh_x{i}")
                    if i >= 1 and f"le_{i-1}" not in ms.out:
                        lnx_evac(v, i - 1)

                def kv_evac(j):
                    for kc in range(4):
                        ms.wait(v, sT, f"kmm_{j}_{kc}")
                        inst = v.tensor_copy(
                            kfm[:, kc * 2048 + j * 512:kc * 2048 + j * 512 + 512],
                            pKV[kc % 2][:])
                        ms.inc(inst, sV, 1, name=f"kev_{j}_{kc}")
                    for t4 in range(4):
                        ms.wait(v, sT, f"vmm_{j}_{t4}")
                        inst = v.tensor_copy(
                            vtm[:, (j * 4 + t4) * 512:(j * 4 + t4) * 512 + 512],
                            pKV[t4 % 2][:])
                        ms.inc(inst, sV, 1, name=f"vev_{j}_{t4}")

                for i in range(8):
                    ln_tile(i)
                lnx_evac(v, 7)
                if KCUT >= 2:
                    kv_evac(0)
                    kv_evac(1)
                for qc in range(16 if KCUT >= 3 else 0):
                    for jb in range(2):
                        ms.wait(v, sT, f"qmm_{qc}_{jb}")
                        if qc >= 2 and jb == 0:
                            ms.wait(v, sQS[qc % 2], f"qsp_{qc-2}")
                            ms.wait(v, sA, f"qsq_{qc-2}")
                        dst = QEV[:, (qc % 2) * 1024 + jb * 512:
                                  (qc % 2) * 1024 + jb * 512 + 512]
                        inst = v.tensor_copy(dst, pQ[jb][:])
                        nm = f"qev_{qc}_{jb}" if jb == 1 else None
                        ms.inc(inst, sV, 1, name=nm)
                for i in range(8, 12):
                    ln_tile(i)
                lnx_evac(v, 11)
                if KCUT >= 2:
                    kv_evac(2)
                for i in range(12, 16):
                    ln_tile(i)
                lnx_evac(v, 15)
                if KCUT >= 2:
                    kv_evac(3)
                if KCUT < 4:
                    return
                for half in range(2):
                    ms.wait(v, sT, f"qbc_{half}")
                    inst = v.tensor_copy(QBC[:, half * 512:half * 512 + 512], pQ[0][:])
                    ms.inc(inst, sV, 1, name=f"qbcc_{half}")
                for sh in range(4):
                    ms.wait(v, sT, f"kbc_{sh}")
                    inst = v.tensor_copy(XT[:, sh * 512:sh * 512 + 512], pQ[1][:])
                    ms.inc(inst, sV, 1, name=f"kbcc_{sh}")
                for kc in range(4):
                    kch = kfm[:, kc * 2048:kc * 2048 + 2048]
                    inst = v.tensor_tensor(RT[:, 0:2048], kch, XT[:, 0:2048], ALU.mult)
                    ms.inc(inst, sV, 1, name=f"krn_{kc}")
                    v.tensor_tensor(H2[:], RT[:, 0:2048], cosk, ALU.mult)
                    ms.wait(v, sKSW, f"ksw_{kc}")
                    v.tensor_tensor(RT[:, 0:2048], SQB[:], sink, ALU.mult)
                    inst = v.tensor_tensor(kch, H2[:], RT[:, 0:2048], ALU.add)
                    ms.inc(inst, sV, 1, name=f"krope_{kc}")
                ms.mark(sV, "krope")

            @blk.scalar
            def _(a):
                for i in range(8):
                    xs = XT[:, (i % 2) * C:(i % 2) * C + C]
                    ln_stats_a(a, xs, f"x{i}")
                for qc in range(16 if KCUT >= 3 else 0):
                    ms.wait(a, sV, f"qev_{qc}_1")
                    if qc >= 1:
                        ms.wait(a, sT, f"qsum_{qc-1}")
                    inst = a.activation(SQB[:, 0:1024],
                                        QEV[:, (qc % 2) * 1024:(qc % 2) * 1024 + 1024],
                                        AF.Square)
                    ms.inc(inst, sA, 1, name=f"qsq_{qc}")
                if KCUT >= 3:
                    ms.wait(a, sT, "qsumF")
                for i in range(8, 16):
                    xs = XT[:, (i % 2) * C:(i % 2) * C + C]
                    ln_stats_a(a, xs, f"x{i}")
                if KCUT < 4:
                    return
                for half in range(2):
                    ms.wait(a, sT, "qsumF")
                    asyncf(a, a.activation(ROWS[0:1, 2048:2560],
                                           pR[half][0:1, :], AF.Ln,
                                           bias=EPSB[0:1, 0:1], scale=1.0 / C))
                    inst = a.activation(ROWS[0:1, half * 512:half * 512 + 512],
                                        ROWS[0:1, 2048:2560],
                                        AF.Exp, scale=-0.5)
                    ms.inc(inst, sA, 1, name=f"qrow_{half}")
                for ph in range(2):
                    for kc in range(4):
                        ms.wait(a, sV, f"kev_3_{kc}")
                        if not (ph == 0 and kc == 0):
                            pv = f"ksum_{ph}_{kc-1}" if kc >= 1 else "ksumF_0"
                            ms.wait(a, sT, pv)
                        inst = a.activation(SQB[:, 0:2048],
                                            kfm[:, kc * 2048:kc * 2048 + 2048],
                                            AF.Square)
                        ms.inc(inst, sA, 1, name=f"ksq_{ph}_{kc}")
                    for sh2 in range(2):
                        sh = ph * 2 + sh2
                        ms.wait(a, sT, f"ksumF_{ph}")
                        r = ROWS[0:1, 1024 + sh * 512:1024 + sh * 512 + 512]
                        a.activation(r, pR[sh2][0:1, :], AF.Ln,
                                     bias=EPSB[0:1, 0:1], scale=1.0 / KC)
                        inst = a.activation(r, r, AF.Exp, scale=-0.5)
                        ms.inc(inst, sA, 1, name=f"krow_{sh}")

            @blk.tensor
            def _(t):
                ms.wait(t, sCONST, "const")
                ms.wait(t, sWKV, "wkv")

                def tile_trans(i):
                    ms.wait(t, sV, f"lh_x{i}")
                    ms.wait(t, sV, f"le_{i-1}")
                    last = None
                    for gi in range(2):
                        for j in range(8):
                            cch = gi * 8 + j
                            last = t.transpose(pT[gi][:, j * P:(j + 1) * P],
                                               H2[:, cch * P:(cch + 1) * P], s_id[:])
                    ms.inc(last, sT, 1, name=f"lt_{i}")

                def kv_block(j):
                    for kc in range(4):
                        if j >= 1:
                            ms.wait(t, sV, f"kev_{j-1}_{kc}")
                            ms.wait(t, sV, f"vev_{j-1}_{2 + (kc % 2)}")
                        lastm = None
                        for cch in range(16):
                            lastm = t.matmul(
                                pKV[kc % 2][:],
                                A4[:, cch * 512 + kc * 128:cch * 512 + kc * 128 + 128],
                                lnx(j)[:, cch * 512:cch * 512 + 512],
                                start=(cch == 0), stop=(cch == 15))
                        ms.inc(lastm, sT, 1, name=f"kmm_{j}_{kc}")
                    for t4 in range(4):
                        if j >= 1:
                            ms.wait(t, sV, f"vev_{j-1}_{2 + (t4 % 2)}")
                        ms.wait(t, sV, f"kev_{j}_{2 + (t4 % 2)}")
                        lastm = None
                        for cch in range(16):
                            lastm = t.matmul(
                                pKV[t4 % 2][:],
                                lnx(j)[:, cch * 512 + t4 * 128:cch * 512 + t4 * 128 + 128],
                                A6[:, cch * 512:cch * 512 + 512],
                                start=(cch == 0), stop=(cch == 15))
                        ms.inc(lastm, sT, 1, name=f"vmm_{j}_{t4}")

                for i in range(8):
                    tile_trans(i)
                for j in (0, 1):
                    ms.wait(t, sV, f"le_{j * 4 + 3}")
                    kv_block(j)
                def qsum_mm(qs):
                    ms.wait(t, sA, f"qsq_{qs}")
                    lastm = None
                    for half in range(2):
                        lastm = t.matmul(pR[half][0:1, :], s_oc[:, 0:1],
                                         SQB[:, half * 512:half * 512 + 512],
                                         start=(qs == 0), stop=(qs == 15))
                    nm = "qsumF" if qs == 15 else f"qsum_{qs}"
                    ms.inc(lastm, sT, 1, name=nm)

                for qc in range(16 if KCUT >= 3 else 0):
                    ms.wait(t, sWQ[qc % 2], f"wq_{qc}")
                    if qc >= 1:
                        ms.wait(t, sV, f"qev_{qc-1}_1")
                    lastm = None
                    for cch in range(16):
                        for jb in range(2):
                            lastm = t.matmul(
                                pQ[jb][:],
                                WQB[:, (qc % 2) * 2048 + cch * 128:
                                    (qc % 2) * 2048 + cch * 128 + 128],
                                lnx(jb)[:, cch * 512:cch * 512 + 512],
                                start=(cch == 0), stop=(cch == 15))
                    ms.inc(lastm, sT, 1, name=f"qmm_{qc}_1")
                    ms.mark(sT, f"qmm_{qc}_0")
                    if qc >= 1:
                        qsum_mm(qc - 1)
                if KCUT >= 3:
                    qsum_mm(15)
                for i in range(8, 12):
                    tile_trans(i)
                if KCUT >= 2:
                    ms.wait(t, sV, "le_11")
                    kv_block(2)
                for i in range(12, 16):
                    tile_trans(i)
                if KCUT >= 2:
                    ms.wait(t, sV, "le_15")
                    kv_block(3)
                if KCUT < 4:
                    return
                for half in range(2):
                    ms.wait(t, sA, f"qrow_{half}")
                    if half >= 1:
                        ms.wait(t, sV, "qbcc_0")
                    rows = ROWS[0:1, half * 512:half * 512 + 512]
                    lastm = t.matmul(pQ[0][:], s_orf[0:1, :], rows,
                                     start=True, stop=True)
                    ms.inc(lastm, sT, 1, name=f"qbc_{half}")
                for ph in range(2):
                    for kc in range(4):
                        ms.wait(t, sA, f"ksq_{ph}_{kc}")
                        lastm = None
                        for sh2 in range(2):
                            sh = ph * 2 + sh2
                            lastm = t.matmul(pR[sh2][0:1, :], s_oc[:, 0:1],
                                             SQB[:, sh * 512:sh * 512 + 512],
                                             start=(kc == 0), stop=(kc == 3))
                        nm = f"ksumF_{ph}" if kc == 3 else f"ksum_{ph}_{kc}"
                        ms.inc(lastm, sT, 1, name=nm)
                    for sh2 in range(2):
                        sh = ph * 2 + sh2
                        ms.wait(t, sA, f"krow_{sh}")
                        if sh >= 1:
                            ms.wait(t, sV, f"kbcc_{sh-1}")
                        rows = ROWS[0:1, 1024 + sh * 512:1024 + sh * 512 + 512]
                        lastm = t.matmul(pQ[1][:], s_orf[0:1, :], rows,
                                         start=True, stop=True)
                        ms.inc(lastm, sT, 1, name=f"kbc_{sh}")

        ps1.close()

        if KLV < 2:
            return nc
        psD = contextlib.ExitStack()
        pDEC = [psD.enter_context(nc.psum_tensor(f"pDEC{i}", [P, 1024], BF16))
                for i in range(2)]

        ps2 = contextlib.ExitStack()
        pL = [ps2.enter_context(nc.psum_tensor(f"pL{i}", [P, 512], F32))
              for i in range(2)]
        pS = ps2.enter_context(nc.psum_tensor("pS", [1, 512], F32))
        pE = [ps2.enter_context(nc.psum_tensor(f"pE{i}", [P, 512], F32))
              for i in range(2)]
        pB = ps2.enter_context(nc.psum_tensor("pB", [P, 512], F32))

        with nc.Block() as blk:
            @blk.sync
            def _(s):
                for hd in range(16):
                    if hd >= 3:
                        ms.wait(s, sT, f"lg_{(hd - 3) * 2 + 1}_15")
                        if KDBG:
                            ms.wait(s, sCONST, f"dbq_{hd-3}")
                    else:
                        ms.wait(s, sQS[1], "qsp_15")
                        ms.wait(s, sA, "qsq_15")
                        ms.wait(s, sV, "krope")
                    inst = s.dma_start(out=qevsl(hd),
                                       in_=q_sp.ap()[hd * P:(hd + 1) * P, :])
                    ms.inc(inst, sQH[hd % 2], 16)
                    qsw = rtsl(hd)
                    ms.inc(s.dma_start(out=qsw[0:64, :],
                                       in_=q_sp.ap()[hd * P + 64:hd * P + 128, :]),
                           sQH[hd % 2], 16)
                    inst = s.dma_start(out=qsw[64:128, :],
                                       in_=q_sp.ap()[hd * P:hd * P + 64, :])
                    ms.inc(inst, sQH[hd % 2], 16, name=f"qh_{hd}")
                    if KDBG:
                        ms.wait(s, sV, f"qp_{hd}")
                        inst = s.dma_start(
                            out=d_qp[hd * P:(hd + 1) * P, :],
                            in_=qevsl(hd))
                        ms.inc(inst, sCONST, 16, name=f"dbq_{hd}")
                if KDBG:
                    ms.wait(s, sV, "krope")
                    ms.inc(s.dma_start(out=d_k, in_=kfm[:, 0:8192]), sCONST, 16)
                    ms.inc(s.dma_start(out=d_v, in_=vtm[:, 0:8192]), sCONST, 16)
                    ms.inc(s.dma_start(out=d_q, in_=q_sp.ap()), sCONST, 16)
                    ms.wait(s, sA, "ex_0_15")
                    ms.inc(s.dma_start(out=d_eb, in_=EB[:, 0:8192]), sCONST, 16)

            @blk.vector
            def _(v):
                def qprep(hd):
                    if hd < 0 or hd >= 16:
                        return
                    ms.wait(v, sQH[hd % 2], f"qh_{hd}")
                    sl = qevsl(hd)
                    qsw = rtsl(hd)
                    v.tensor_tensor(H2[:, 0:1024], sl, cosq, ALU.mult)
                    v.tensor_tensor(SQB[:, 0:1024], qsw, sinq, ALU.mult)
                    v.tensor_tensor(SQB[:, 1024:2048], H2[:, 0:1024], SQB[:, 0:1024],
                                    ALU.add)
                    inst = v.tensor_tensor(sl, SQB[:, 1024:2048], QBC[:], ALU.mult)
                    ms.inc(inst, sV, 1, name=f"qp_{hd}")

                qprep(0)
                qprep(1)
                for it in range(32):
                    hd, tt = it // 2, it % 2
                    if tt == 1:
                        qprep(hd + 2)
                    if it >= 1:
                        pit = it - 1
                        phd = pit // 2
                        ms.wait(v, sA, f"rb_{pit}")
                        dst = A3[:, phd * 1024 + (pit % 2) * 512:
                                 phd * 1024 + (pit % 2) * 512 + 512]
                        inst = v.tensor_tensor(
                            dst, pE[pit % 2][:],
                            RBS[:, (pit % 2) * 512:(pit % 2) * 512 + 512], ALU.mult)
                        ms.inc(inst, sV, 1, name=f"encv_{pit}")
                    pump_v(v, pDEC, it * 2)
                    pump_v(v, pDEC, it * 2 + 1)
                pit = 31
                ms.wait(v, sA, f"rb_{pit}")
                dst = A3[:, 15 * 1024 + 512:15 * 1024 + 1024]
                inst = v.tensor_tensor(dst, pE[pit % 2][:],
                                       RBS[:, (pit % 2) * 512:(pit % 2) * 512 + 512],
                                       ALU.mult)
                ms.inc(inst, sV, 1, name=f"encv_{pit}")

            @blk.scalar
            def _(a):
                def wcv_load(i):
                    if i < 0 or i >= 32:
                        return
                    ch, h = i // 2, i % 2
                    inst = a.dma_start(
                        out=XT[:, (i % 2) * 1024:(i % 2) * 1024 + 1024],
                        in_=wo[ch * P:(ch + 1) * P, h * 1024:(h + 1) * 1024])
                    ms.inc(inst, sO1[0], 16, name=f"wcl_{i}")

                wcv_load(0)
                wcv_load(1)
                for it in range(32):
                    for stt in range(16):
                        ms.wait(a, sT, f"lg_{it}_{stt}")
                        inst = a.activation(EB[:, stt * 512:stt * 512 + 512],
                                            pL[stt % 2][:], AF.Exp)
                        ms.inc(inst, sA, 1, name=f"ex_{it}_{stt}")
                    ms.wait(a, sT, f"ss_{it}")
                    asyncf(a, a.activation(ROWS[0:1, 512:1024], pS[0:1, :], AF.Ln))
                    inst = a.activation(ROWS[0:1, 0:512], ROWS[0:1, 512:1024],
                                        AF.Exp, scale=-1.0)
                    ms.inc(inst, sA, 1, name=f"rr_{it}")
                    ms.wait(a, sT, f"pb_{it}")
                    inst = a.activation(RBS[:, (it % 2) * 512:(it % 2) * 512 + 512],
                                        pB[:], AF.Copy)
                    ms.inc(inst, sA, 1, name=f"rb_{it}")
                    ms.wait(a, sO1[0], f"wcl_{it}")
                    if it >= 4:
                        ms.wait(a, sO1[1], f"wcs_{it-4}")
                    inst = a.activation(
                        WQB[:, (it % 4) * 1024:(it % 4) * 1024 + 1024],
                        XT[:, (it % 2) * 1024:(it % 2) * 1024 + 1024], AF.Copy)
                    ms.inc(inst, sA, 1, name=f"wcc_{it}")
                    wcv_load(it + 2)

            @blk.tensor
            def _(t):
                for it in range(32):
                    hd, tt = it // 2, it % 2
                    kh = hd // 4
                    if tt == 0:
                        ms.wait(t, sV, f"qp_{hd}")
                    if it >= 2:
                        ms.wait(t, sV, f"encv_{it-2}")
                    qsl = qevsl(hd)[:, tt * 512:tt * 512 + 512]
                    for stt in range(16):
                        if stt >= 2:
                            ms.wait(t, sA, f"ex_{it}_{stt-2}")
                        elif it >= 1:
                            ms.wait(t, sA, f"ex_{it-1}_{14 + stt}")
                        lastm = t.matmul(
                            pL[stt % 2][:],
                            kfm[:, kh * 2048 + stt * P:kh * 2048 + stt * P + P],
                            qsl, start=True, stop=True)
                        ms.inc(lastm, sT, 1, name=f"lg_{it}_{stt}")
                        if stt >= 1:
                            ms.wait(t, sA, f"ex_{it}_{stt-1}")
                            eb = EB[:, (stt - 1) * 512:stt * 512]
                            t.matmul(pS[0:1, :], s_oc[:, 0:1], eb,
                                     start=(stt == 1), stop=False)
                            t.matmul(pE[it % 2][:],
                                     vtm[:, (stt - 1) * 512 + kh * P:
                                         (stt - 1) * 512 + kh * P + P],
                                     eb, start=(stt == 1), stop=False)
                    ms.wait(t, sA, f"ex_{it}_15")
                    eb = EB[:, 15 * 512:16 * 512]
                    lastm = t.matmul(pS[0:1, :], s_oc[:, 0:1], eb,
                                     start=False, stop=True)
                    ms.inc(lastm, sT, 1, name=f"ss_{it}")
                    lastm = t.matmul(pE[it % 2][:],
                                     vtm[:, 15 * 512 + kh * P:15 * 512 + kh * P + P],
                                     eb, start=False, stop=True)
                    ms.inc(lastm, sT, 1, name=f"pv_{it}")
                    ms.wait(t, sA, f"rr_{it}")
                    if it >= 1:
                        ms.wait(t, sA, f"rb_{it-1}")
                    lastm = t.matmul(pB[:], s_orf[0:1, :], ROWS[0:1, 0:512],
                                     start=True, stop=True)
                    ms.inc(lastm, sT, 1, name=f"pb_{it}")
                    pump_t(t, pDEC, it * 2)
                    pump_t(t, pDEC, it * 2 + 1)

            @blk.gpsimd
            def _(g):
                def wcv_store(i):
                    if i < 0 or i >= 32:
                        return
                    ch, h = i // 2, i % 2
                    ms.wait(g, sA, f"wcc_{i}")
                    dstv = wo_bf.ap().rearrange("p (q x) -> p q x", q=4)[
                        :, 2 * h:2 * h + 2, ch * 512:(ch + 1) * 512]
                    inst = g.dma_start(
                        out=dstv,
                        in_=WQB[:, (i % 4) * 1024:(i % 4) * 1024 + 1024
                                ].rearrange("p (a b) -> p a b", b=512))
                    ms.inc(inst, sO1[1], 16, name=f"wcs_{i}")

                for it in range(32):
                    pump_g(g, it * 2)
                    pump_g(g, it * 2 + 1)
                    wcv_store(it - 4)
                for i in range(28, 32):
                    wcv_store(i)
                ms.mark(sO1[1], "wocvD")

        ps2.close()
        if KLV < 3:
            return nc

        ps3 = contextlib.ExitStack()
        pW = [ps3.enter_context(nc.psum_tensor(f"pW{i}", [P, 512], F32))
              for i in range(4)]
        pT3 = [ps3.enter_context(nc.psum_tensor(f"pT3{i}", [P, 1024], BF16))
               for i in range(2)]

        def wsl3(i):
            return A5 if i % 2 == 0 else A6


        def ht3(t, tt8):
            if tt8 < 0 or tt8 > 7:
                return
            ms.wait(t, sV, f"lh2_{tt8}")
            if tt8 >= 1:
                ms.wait(t, sV, f"he_{tt8-1}")
            last = None
            for gi in range(2):
                for j in range(8):
                    cch = gi * 8 + j
                    last = t.transpose(pT3[gi][:, j * P:(j + 1) * P],
                                       H2[:, cch * P:(cch + 1) * P], s_id[:])
            ms.inc(last, sT, 1, name=f"ht_{tt8}")

        def he3(v, tt8):
            if tt8 < 0 or tt8 > 7:
                return
            ms.wait(v, sT, f"ht_{tt8}")
            dst = A2[:].rearrange("p (a b) -> p a b", b=1024)
            last = None
            for gi in range(2):
                srcp = pT3[gi][:].rearrange("p (a b) -> p a b", b=P)
                d3 = dst[:, gi * 8:gi * 8 + 8, tt8 * P:tt8 * P + P]
                last = v.tensor_copy(d3, srcp)
            ms.inc(last, sV, 1, name=f"he_{tt8}")

        with nc.Block() as blk:
            @blk.sync
            def _(s):
                for jp in range(4):
                    for t2 in range(2):
                        tt8 = 2 * jp + t2
                        if jp == 0 and t2 == 0:
                            ms.wait(s, sO1[1], "wocvD")
                        if jp >= 1:
                            ms.wait(s, sV, f"lh2_{tt8-2}")
                        inst = s.dma_start(
                            out=XT[:, t2 * 2048:(t2 + 1) * 2048],
                            in_=xo[tt8 * P:(tt8 + 1) * P, :])
                        ms.inc(inst, sXC, 16, name=f"xl_{tt8}")
                    for q in range(4):
                        i = jp * 4 + q
                        if i >= 2:
                            ms.wait(s, sT, f"wom_{i-2}_1")
                        inst = s.dma_start(
                            out=wsl3(i)[:],
                            in_=wo_bf.ap()[:, q * 8192:(q + 1) * 8192])
                        ms.inc(inst, sWO[i % 2], 16, name=f"wol_{i}")
                    for t2 in range(2):
                        tt8 = 2 * jp + t2
                        ms.wait(s, sV, f"oev_{tt8}_3")
                        inst = s.dma_start(out=out[tt8 * P:(tt8 + 1) * P, :],
                                           in_=XT[:, t2 * 2048:(t2 + 1) * 2048])
                        ms.inc(inst, sXR[t2], 16, name=f"xw_{tt8}")
                ms.mark(sXR[0], "xwD0")
                ms.mark(sXR[1], "xwD1")
                if KDBG:
                    ms.wait(s, sAG, "ag_g")
                    ms.inc(s.dma_start(out=d_wg, in_=ago["g"].ap()), sCONST, 16)
                    ms.wait(s, sV, "encv_31")
                    ms.inc(s.dma_start(out=d_enc, in_=A3[:, 0:16384]), sCONST, 16)
                    ms.wait(s, sV, "he_7")
                    ms.inc(s.dma_start(out=d_h, in_=A2[:, 0:16384]), sCONST, 16)

            @blk.tensor
            def _(t):
                for jp in range(4):
                    for q in range(4):
                        i = jp * 4 + q
                        if q == 0:
                            ht3(t, 2 * jp - 2)
                        if q == 2:
                            ht3(t, 2 * jp - 1)
                        ms.wait(t, sWO[i % 2], f"wol_{i}")
                        for t2 in range(2):
                            tt8 = 2 * jp + t2
                            k = i * 2 + t2
                            pb = pW[k % 4]
                            if k >= 4:
                                ip = i - 2
                                ms.wait(t, sV,
                                        f"oev_{2 * (ip // 4) + t2}_{ip % 4}")
                            for ch in range(16):
                                t.matmul(pb[:, 0:512],
                                         A3[:, ch * 1024 + tt8 * P:
                                            ch * 1024 + tt8 * P + P],
                                         wsl3(i)[:, ch * 512:(ch + 1) * 512],
                                         start=(ch == 0), stop=False)
                            ms.wait(t, sXC, f"xl_{tt8}")
                            if i == 0 and t2 == 0:
                                ms.wait(t, sA, "idf3")
                            lastm = t.matmul(pb[:, 0:512], RBS[:, 0:128].bitcast(F32),
                                             XT[:, t2 * 2048 + q * 512:
                                                t2 * 2048 + q * 512 + 512],
                                             start=False, stop=True)
                            ms.inc(lastm, sT, 1, name=f"wom_{i}_{t2}")
                        pump_t(t, pDEC, 64 + i * 2)
                        pump_t(t, pDEC, 64 + i * 2 + 1)
                        if i == 15:
                            pump_t(t, pDEC, 96)
                            pump_t(t, pDEC, 97)
                ht3(t, 6)
                ht3(t, 7)

            @blk.vector
            def _(v):
                for jp in range(4):
                    for q in range(4):
                        i = jp * 4 + q
                        for t2 in range(2):
                            tt8 = 2 * jp + t2
                            k = i * 2 + t2
                            ms.wait(v, sT, f"wom_{i}_{t2}")
                            inst = v.tensor_copy(
                                XT[:, t2 * 2048 + q * 512:
                                   t2 * 2048 + q * 512 + 512],
                                pW[k % 4][:, 0:512])
                            ms.inc(inst, sV, 1, name=f"oev_{tt8}_{q}")
                        pump_v(v, pDEC, 64 + i * 2)
                        pump_v(v, pDEC, 64 + i * 2 + 1)
                        if i == 15:
                            pump_v(v, pDEC, 96)
                            pump_v(v, pDEC, 97)
                        if q == 0:
                            he3(v, 2 * jp - 2)
                        if q == 2:
                            he3(v, 2 * jp - 1)
                    for t2 in range(2):
                        tt8 = 2 * jp + t2
                        xr = XT[:, t2 * 2048:(t2 + 1) * 2048]
                        ln_stats_v(v, xr, f"r{tt8}")
                        ms.wait(v, sA, f"lr_r{tt8}")
                        ms.wait(v, sT, f"ht_{tt8-1}")
                        inst = v.tensor_scalar(H2[:], xr, STA[:, 1:2],
                                               STA[:, 5:6],
                                               op0=ALU.subtract, op1=ALU.mult)
                        ms.inc(inst, sV, 1, name=f"lh2_{tt8}")
                he3(v, 6)
                he3(v, 7)

            @blk.scalar
            def _(a):
                inst = a.activation(RBS[:, 0:128].bitcast(F32), s_id[:], AF.Copy)
                ms.inc(inst, sA, 1, name="idf3")
                for tt8 in range(8):
                    xr = XT[:, (tt8 % 2) * 2048:(tt8 % 2) * 2048 + 2048]
                    ln_stats_a(a, xr, f"r{tt8}")

            @blk.gpsimd
            def _(g):
                for i in range(16):
                    pump_g(g, 64 + i * 2)
                    pump_g(g, 64 + i * 2 + 1)
                    if i == 2:
                        ag_trig(g, "u")
                    if i == 10:
                        ag_trig(g, "d1")
                    if i == 15:
                        pump_g(g, 96)
                        pump_g(g, 97)

        ps3.close()
        if KLV < 4:
            return nc

        with nc.Block() as blk:
            @blk.vector
            def _(v):
                for p in range(98, NPUMP):
                    pump_v(v, pDEC, p)

            @blk.tensor
            def _(t):
                for p in range(98, NPUMP):
                    pump_t(t, pDEC, p)

            @blk.gpsimd
            def _(g):
                for p in range(98, NPUMP):
                    pump_g(g, p)
                ag_trig(g, "d2")

        psD.close()
        if KLV < 5:
            return nc

        ps5 = contextlib.ExitStack()
        pG = ps5.enter_context(nc.psum_tensor("pG", [P, 1024], F32))
        pU = ps5.enter_context(nc.psum_tensor("pU", [P, 1024], F32))
        pDn = [ps5.enter_context(nc.psum_tensor(f"pDn{i}", [P, 512], F32))
               for i in range(2)]

        def isl(gc):
            if gc < 8:
                return A5[:, gc * 1024:gc * 1024 + 1024]
            return A6[:, (gc - 8) * 1024:(gc - 8) * 1024 + 1024]

        def accL(b, cc, tt8):
            return b * 32 + cc * 8 + tt8

        def accN(L):
            b, r = divmod(L, 32)
            cc, tt8 = divmod(r, 8)
            return f"acc_{b}_{cc}_{tt8}"

        with nc.Block() as blk:
            if KDBG:
                @blk.sync
                def _(s):
                    ms.wait(s, sAG, "ag_d1")
                    ms.wait(s, sAG, "ag_d2")
                    ms.inc(s.dma_start(out=d_wd[:, 0:1024], in_=ago_dh[0].ap()),
                           sCONST, 16)
                    ms.inc(s.dma_start(out=d_wd[:, 1024:2048], in_=ago_dh[1].ap()),
                           sCONST, 16)

            @blk.sync
            def _(s):
                for g in range(128):
                    if g == 0:
                        ms.wait(s, sAG, "ag_g")
                        ms.wait(s, sAG, "ag_u")
                        ms.wait(s, sV, "dv_95")
                        ms.wait(s, sT, "pv_31")
                    if g >= 2:
                        ms.wait(s, sT, f"um_{g - 2}")
                    rank, rtl = g // 16, g % 16
                    for m, arr, base in (("g", ago["g"], 0), ("u", ago["u"], 4096)):
                        srcw = arr.ap().rearrange("a (p f) -> a p f", p=P)[
                            rank * 16 + rtl, :, :]
                        dst = A4[:, base + (g % 2) * 2048:base + (g % 2) * 2048 + 2048]
                        inst = s.dma_start(out=dst, in_=srcw)
                        ms.inc(inst, sWG[g % 2], 16, name=f"w{m}l_{g}")

            @blk.scalar
            def _(a):
                for b in range(8):
                    for gc in range(16):
                        g = b * 16 + gc
                        ms.wait(a, sT, f"gm_{g}")
                        if g >= 2:
                            g2 = g - 2
                            ms.wait(a, sV, f"int_{g2 // 16}_{g2 % 16}")
                        inst = a.activation(A7[:, (g % 2) * 1024:(g % 2) * 1024 + 1024],
                                            pG[:], AF.Gelu_apprx_tanh)
                        ms.inc(inst, sA, 1, name=f"gg_{g}")
                        ms.wait(a, sT, f"um_{g}")
                        inst = a.activation(A7[:, 2048 + (g % 2) * 1024:
                                               2048 + (g % 2) * 1024 + 1024],
                                            pU[:], AF.Copy)
                        ms.inc(inst, sA, 1, name=f"gu_{g}")

            @blk.tensor
            def _(t):
                for b in range(8):
                    for gc in range(16):
                        g = b * 16 + gc
                        ms.wait(t, sWG[g % 2], f"wgl_{g}")
                        ms.wait(t, sWG[g % 2], f"wul_{g}")
                        if g == 0:
                            ms.wait(t, sV, "he_7")
                        if g >= 1:
                            ms.wait(t, sA, f"gg_{g-1}")
                            ms.wait(t, sA, f"gu_{g-1}")
                        wgs = A4[:, (g % 2) * 2048:(g % 2) * 2048 + 2048]
                        wus = A4[:, 4096 + (g % 2) * 2048:4096 + (g % 2) * 2048 + 2048]
                        lastm = None
                        for cch in range(16):
                            for tt2 in range(2):
                                lastm = t.matmul(
                                    pG[:, tt2 * 512:tt2 * 512 + 512],
                                    wgs[:, cch * 128:cch * 128 + 128],
                                    A2[:, cch * 1024 + tt2 * 512:
                                       cch * 1024 + tt2 * 512 + 512],
                                    start=(cch == 0), stop=(cch == 15))
                        ms.inc(lastm, sT, 1, name=f"gm_{g}")
                        lastm = None
                        for cch in range(16):
                            for tt2 in range(2):
                                lastm = t.matmul(
                                    pU[:, tt2 * 512:tt2 * 512 + 512],
                                    wus[:, cch * 128:cch * 128 + 128],
                                    A2[:, cch * 1024 + tt2 * 512:
                                       cch * 1024 + tt2 * 512 + 512],
                                    start=(cch == 0), stop=(cch == 15))
                        ms.inc(lastm, sT, 1, name=f"um_{g}")
                    ms.wait(t, sV, f"int_{b}_15")
                    for cc in range(4):
                        ms.wait(t, sWD[(b * 4 + cc) % 2], f"wdl_{b}_{cc}")
                        for tt8 in range(8):
                            i = cc * 8 + tt8
                            if i >= 2:
                                pi = i - 2
                                ms.wait(t, sV, f"o2e_{b}_{pi // 8}_{pi % 8}")
                            lastm = None
                            for gc in range(16):
                                lastm = t.matmul(
                                    pDn[i % 2][:],
                                    isl(gc)[:, tt8 * P:tt8 * P + P],
                                    A3[:, ((b * 4 + cc) % 2) * 8192 + gc * 512:
                                       ((b * 4 + cc) % 2) * 8192 + gc * 512 + 512],
                                    start=(gc == 0), stop=(gc == 15))
                            ms.inc(lastm, sT, 1, name=f"dn_{b}_{cc}_{tt8}")

            @blk.vector
            def _(v):
                for b in range(8):
                    for gc in range(16):
                        g = b * 16 + gc
                        ms.wait(v, sA, f"gg_{g}")
                        ms.wait(v, sA, f"gu_{g}")
                        if b >= 1 and gc == 0:
                            ms.wait(v, sT, f"dn_{b-1}_3_7")
                        inst = v.tensor_tensor(
                            isl(gc), A7[:, (g % 2) * 1024:(g % 2) * 1024 + 1024],
                            A7[:, 2048 + (g % 2) * 1024:2048 + (g % 2) * 1024 + 1024],
                            ALU.mult)
                        ms.inc(inst, sV, 1, name=f"int_{b}_{gc}")
                    for cc in range(4):
                        for tt8 in range(8):
                            i = cc * 8 + tt8
                            L = accL(b, cc, tt8)
                            ms.wait(v, sT, f"dn_{b}_{cc}_{tt8}")
                            if L >= 2:
                                ms.wait(v, sAC[L % 2], accN(L - 2))
                            o2 = A7[:, 4096 + (i % 2) * 1024:4096 + (i % 2) * 1024 + 1024]
                            inst = v.tensor_copy(o2.bitcast(F32), pDn[i % 2][:])
                            ms.inc(inst, sV, 1, name=f"o2e_{b}_{cc}_{tt8}")

            @blk.gpsimd
            def _(g):
                def wds_load(g, b, cc):
                    if b > 7:
                        return
                    idx = b * 4 + cc
                    if idx == 0:
                        ms.wait(g, sAG, "ag_d1")
                        ms.wait(g, sT, "wom_15_1")
                    if idx == 2:
                        ms.wait(g, sAG, "ag_d2")
                    if idx >= 2:
                        pcc = idx - 2
                        ms.wait(g, sT, f"dn_{pcc // 4}_{pcc % 4}_7")
                    lastd = None
                    for gc2 in range(16):
                        srcd = ago_dh[cc // 2].ap()[
                            (b * 16 + gc2) * P:(b * 16 + gc2 + 1) * P,
                            (cc % 2) * 512:(cc % 2) * 512 + 512]
                        dst = A3[:, (idx % 2) * 8192 + gc2 * 512:
                                 (idx % 2) * 8192 + gc2 * 512 + 512]
                        lastd = g.dma_start(out=dst, in_=srcd)
                        ms.inc(lastd, sWD[idx % 2], 16)
                    ms.mark(sWD[idx % 2], f"wdl_{b}_{cc}")

                ms.wait(g, sXR[0], "xwD0")
                ms.wait(g, sXR[1], "xwD1")
                wds_load(g, 0, 0)
                wds_load(g, 0, 1)
                for b in range(8):
                    for cc in range(4):
                        for tt8 in range(8):
                            i = cc * 8 + tt8
                            L = accL(b, cc, tt8)
                            ms.wait(g, sV, f"o2e_{b}_{cc}_{tt8}")
                            if b >= 1:
                                ms.wait(g, sAC[L % 2], f"acc_{b-1}_{cc}_{tt8}")
                            o2 = A7[:, 4096 + (i % 2) * 1024:
                                    4096 + (i % 2) * 1024 + 1024]
                            inst = g.dma_start(
                                out=out[tt8 * P:(tt8 + 1) * P, cc * 512:(cc + 1) * 512],
                                in_=o2.bitcast(F32), accum_op=ALU.add)
                            ms.inc(inst, sAC[L % 2], 16, name=f"acc_{b}_{cc}_{tt8}")
                        if cc == 0:
                            wds_load(g, b, 2)
                        elif cc == 1:
                            wds_load(g, b, 3)
                        elif cc == 2:
                            wds_load(g, b + 1, 0)
                        else:
                            wds_load(g, b + 1, 1)
                ms.wait(g, sAC[0], "acc_7_3_6")
                ms.wait(g, sAC[1], "acc_7_3_7")

        ps5.close()
    return nc


_CACHE = {}


def _consts():
    if "c" in _CACHE:
        return _CACHE["c"]
    HH = 128
    fraction = (np.arange(0, HH, 2) / HH).astype(np.float32)
    timescale = (10000.0 ** fraction).astype(np.float32)
    sinus = np.arange(TA, dtype=np.float32)[:, None] * (1.0 / timescale)[None, :]
    sinus = np.concatenate([sinus, sinus], axis=1)
    sin_t = np.sin(sinus).astype(np.float32)
    cos_t = np.cos(sinus).astype(np.float32)
    s = np.float32(1.0 / np.sqrt(HH))
    bf = ml_dtypes.bfloat16
    cosq = (cos_t.T * s).astype(bf)
    sinq = (np.concatenate([-sin_t.T[:64], sin_t.T[64:]], axis=0) * s).astype(bf)
    cosk = cos_t.T.astype(bf)
    sink = np.concatenate([-sin_t.T[:64], sin_t.T[64:]], axis=0).astype(bf)
    ident = np.eye(P).astype(bf)
    onesc = np.ones((P, 1), dtype=bf)
    onesrf = np.ones((1, P), dtype=np.float32)
    _CACHE["c"] = (cosq, sinq, cosk, sink, ident, onesc, onesrf)
    return _CACHE["c"]


def _program():
    if "nc" in _CACHE:
        return _CACHE["nc"]
    ms1 = MS()
    build(ms1)
    ms2 = MS(prev=ms1.out, semof_prev=ms1.semof)
    nc = build(ms2)
    assert ms1.cnt == ms2.cnt, "pass mismatch"
    _CACHE["nc"] = nc
    return nc


def kernel(L0_attn_x, L0_attn_Wq, L0_attn_Wk, L0_attn_Wv, L0_attn_Wout,
           L0_ffn_gate_W_packed, L0_ffn_gate_scales, L0_ffn_up_W_packed,
           L0_ffn_up_scales, L0_ffn_down_W_packed, L0_ffn_down_scales):
    x = np.ascontiguousarray(np.asarray(L0_attn_x, dtype=np.float32))
    wqh = np.ascontiguousarray(np.asarray(L0_attn_Wq, dtype=np.float32))
    wkh = np.ascontiguousarray(np.asarray(L0_attn_Wk, dtype=np.float32))
    wvh = np.ascontiguousarray(np.asarray(L0_attn_Wv, dtype=np.float32))
    woh = np.ascontiguousarray(np.asarray(L0_attn_Wout, dtype=np.float32))
    gpk = np.ascontiguousarray(np.asarray(L0_ffn_gate_W_packed, dtype=np.int32))
    gsc = np.ascontiguousarray(np.asarray(L0_ffn_gate_scales, dtype=np.float32))
    upk = np.ascontiguousarray(np.asarray(L0_ffn_up_W_packed, dtype=np.int32))
    usc = np.ascontiguousarray(np.asarray(L0_ffn_up_scales, dtype=np.float32))
    dpk = np.ascontiguousarray(np.asarray(L0_ffn_down_W_packed, dtype=np.int32))
    dsc = np.ascontiguousarray(np.asarray(L0_ffn_down_scales, dtype=np.float32))
    cosq, sinq, cosk, sink, ident, onesc, onesrf = _consts()
    epsv = np.full((P, 1), 1e-6, dtype=np.float32)

    nc = _program()
    in_maps = []
    for c in range(8):
        b, th = c // 2, c % 2
        xown = np.ascontiguousarray(x[b, th * TO:(th + 1) * TO])
        xoth = np.ascontiguousarray(x[b, (1 - th) * TO:(2 - th) * TO])
        tq = np.ascontiguousarray(np.concatenate(
            [cosq[:, th * TO:(th + 1) * TO], sinq[:, th * TO:(th + 1) * TO]], axis=1))
        ckp = np.concatenate([cosk[:, th * TO:(th + 1) * TO],
                              cosk[:, (1 - th) * TO:(2 - th) * TO]], axis=1)
        skp = np.concatenate([sink[:, th * TO:(th + 1) * TO],
                              sink[:, (1 - th) * TO:(2 - th) * TO]], axis=1)
        tk = np.ascontiguousarray(np.concatenate([ckp, skp], axis=1))
        in_maps.append(dict(
            xo=xown, xz=xoth, wq=wqh, wk=wkh, wv=wvh, wo=woh,
            gpk=np.ascontiguousarray(gpk[c * NSL:(c + 1) * NSL]),
            gsc=np.ascontiguousarray(gsc[c * NSL:(c + 1) * NSL]),
            upk=np.ascontiguousarray(upk[c * NSL:(c + 1) * NSL]),
            usc=np.ascontiguousarray(usc[c * NSL:(c + 1) * NSL]),
            dpk=np.ascontiguousarray(dpk[:, c * 1024:(c + 1) * 1024]),
            dsc=np.ascontiguousarray(dsc[:, c * 64:(c + 1) * 64]),
            identd=ident, onescd=onesc, onesrfd=onesrf, epsd=epsv,
            tblq=tq, tblk=tk))
    res = run_bass_kernel_spmd(nc, in_maps, core_ids=list(range(8)))
    outv = np.empty((4, TA, C), dtype=np.float32)
    for c in range(8):
        b, th = c // 2, c % 2
        outv[b, th * TO:(th + 1) * TO] = res.results[c]["out"]
    return outv


# version 20
# speedup vs baseline: 1.0016x; 1.0016x over previous
import contextlib
import os
import numpy as np
import ml_dtypes
import concourse.bass as bass
import concourse.mybir as mybir
from concourse.bass_utils import run_bass_kernel_spmd

F32 = mybir.dt.float32
BF16 = mybir.dt.bfloat16
I32 = mybir.dt.int32
AF = mybir.ActivationFunctionType
ALU = mybir.AluOpType
AX = mybir.AxisListType.X

P = 128
TA = 2048
TO = 1024
C = 2048
KC = 512
MLP = 16384
NSL = 2048
NSUB = 96
NPUMP = 101
KLV = int(os.environ.get("KLV", "9"))
KCUT = int(os.environ.get("KCUT", "9"))
KDBG = os.environ.get("KDBG", "0") == "1"


class Sem:
    def __init__(self, h, name):
        self.h, self.name = h, name


class MS:
    def __init__(self, prev=None, semof_prev=None):
        self.prev = prev if prev is not None else {}
        self.semof_prev = semof_prev if semof_prev is not None else {}
        self.out = {}
        self.cnt = {}
        self.semof = {}

    def inc(self, inst, sem, k=1, name=None):
        inst.then_inc(sem.h, k)
        self.cnt[sem.name] = self.cnt.get(sem.name, 0) + k
        if name is not None:
            assert name not in self.out, name
            self.out[name] = self.cnt[sem.name]
            self.semof[name] = sem.name
        return self.cnt[sem.name]

    def mark(self, sem, name):
        assert name not in self.out, name
        self.out[name] = self.cnt.get(sem.name, 0)
        self.semof[name] = sem.name

    def wait(self, eng, sem, name):
        so = self.semof_prev.get(name)
        assert so is None or so == sem.name, (name, so, sem.name)
        v = self.prev.get(name, 0)
        if v > 0:
            eng.wait_ge(sem.h, v)


def build(ms):
    nc = bass.Bass("TRN2", target_bir_lowering=False, debug=False)
    xo = nc.dram_tensor("xo", [TO, C], F32, kind="ExternalInput").ap()
    xz = nc.dram_tensor("xz", [TO, C], F32, kind="ExternalInput").ap()
    wq = nc.dram_tensor("wq", [C, C], F32, kind="ExternalInput").ap()
    wk = nc.dram_tensor("wk", [C, KC], F32, kind="ExternalInput").ap()
    wv = nc.dram_tensor("wv", [C, KC], F32, kind="ExternalInput").ap()
    wo = nc.dram_tensor("wo", [C, C], F32, kind="ExternalInput").ap()
    pks = {"g": nc.dram_tensor("gpk", [NSL, 1024], I32, kind="ExternalInput").ap(),
           "u": nc.dram_tensor("upk", [NSL, 1024], I32, kind="ExternalInput").ap(),
           "d": nc.dram_tensor("dpk", [C, 1024], I32, kind="ExternalInput").ap()}
    scs = {"g": nc.dram_tensor("gsc", [NSL, 64], F32, kind="ExternalInput").ap(),
           "u": nc.dram_tensor("usc", [NSL, 64], F32, kind="ExternalInput").ap(),
           "d": nc.dram_tensor("dsc", [C, 64], F32, kind="ExternalInput").ap()}
    identd = nc.dram_tensor("identd", [P, P], BF16, kind="ExternalInput").ap()
    onescd = nc.dram_tensor("onescd", [P, 1], BF16, kind="ExternalInput").ap()
    onesrfd = nc.dram_tensor("onesrfd", [1, P], F32, kind="ExternalInput").ap()
    tblq = nc.dram_tensor("tblq", [P, 2 * TO], BF16, kind="ExternalInput").ap()
    epsd = nc.dram_tensor("epsd", [P, 1], F32, kind="ExternalInput").ap()
    tblk = nc.dram_tensor("tblk", [P, 2 * TA], BF16, kind="ExternalInput").ap()
    out = nc.dram_tensor("out", [TO, C], F32, kind="ExternalOutput").ap()
    if KDBG:
        d_q = nc.dram_tensor("d_q", [16 * P, TO], BF16, kind="ExternalOutput").ap()
        d_k = nc.dram_tensor("d_k", [P, 8192], BF16, kind="ExternalOutput").ap()
        d_v = nc.dram_tensor("d_v", [P, 8192], BF16, kind="ExternalOutput").ap()
        d_qp = nc.dram_tensor("d_qp", [16 * P, TO], BF16, kind="ExternalOutput").ap()
        d_enc = nc.dram_tensor("d_enc", [P, 16384], BF16, kind="ExternalOutput").ap()
        d_h = nc.dram_tensor("d_h", [P, 16384], BF16, kind="ExternalOutput").ap()
        d_eb = nc.dram_tensor("d_eb", [P, 8192], BF16, kind="ExternalOutput").ap()
        d_wg = nc.dram_tensor("d_wg", [128, P * 2048], BF16, kind="ExternalOutput").ap()
        d_wd = nc.dram_tensor("d_wd", [MLP, C], BF16, kind="ExternalOutput").ap()

    q_sp = nc.dram_tensor("q_sp", [16 * P, TO], BF16, kind="Internal")
    wo_bf = nc.dram_tensor("wo_bf", [P, 32768], BF16, kind="Internal")
    agi = {"g": nc.dram_tensor("agi_g", [16, P * 2048], BF16, kind="Internal"),
           "u": nc.dram_tensor("agi_u", [16, P * 2048], BF16, kind="Internal")}
    ago = {"g": nc.dram_tensor("ago_g", [128, P * 2048], BF16, kind="Internal"),
           "u": nc.dram_tensor("ago_u", [128, P * 2048], BF16, kind="Internal")}
    agi_dh = [nc.dram_tensor(f"agi_d{h}", [NSL, 1024], BF16, kind="Internal")
              for h in range(2)]
    ago_dh = [nc.dram_tensor(f"ago_d{h}", [MLP, 1024], BF16, kind="Internal")
              for h in range(2)]

    st = contextlib.ExitStack()
    e = st.enter_context
    with st:
        s_id = e(nc.sbuf_tensor("s_id", [P, P], BF16))
        s_oc = e(nc.sbuf_tensor("s_oc", [P, 1], BF16))
        s_orf = e(nc.sbuf_tensor("s_orf", [1, P], F32))
        EPSB = e(nc.sbuf_tensor("EPSB", [P, 1], F32))
        A2 = e(nc.sbuf_tensor("A2", [P, 16384], BF16))
        A3 = e(nc.sbuf_tensor("A3", [P, 16384], BF16))
        A4 = e(nc.sbuf_tensor("A4", [P, 8192], BF16))
        A5 = e(nc.sbuf_tensor("A5", [P, 8192], BF16))
        A6 = e(nc.sbuf_tensor("A6", [P, 8192], BF16))
        A7 = e(nc.sbuf_tensor("A7", [P, 8192], BF16))
        WQB = e(nc.sbuf_tensor("WQB", [P, 4096], BF16))
        QEV = e(nc.sbuf_tensor("QEV", [P, 2048], BF16))
        QBC = e(nc.sbuf_tensor("QBC", [P, 1024], BF16))
        XT = e(nc.sbuf_tensor("XT", [P, 4096], F32))
        SQB = e(nc.sbuf_tensor("SQB", [P, 2048], BF16))
        H2 = e(nc.sbuf_tensor("H2", [P, 2048], BF16))
        RT = e(nc.sbuf_tensor("RT", [P, 2048], BF16))
        STA = e(nc.sbuf_tensor("STA", [P, 16], F32))
        ROWS = e(nc.sbuf_tensor("ROWS", [1, 3072], F32))
        RBS = e(nc.sbuf_tensor("RBS", [P, 1024], F32))
        DPK = e(nc.sbuf_tensor("DPK", [P, 1024], I32))
        DTS = e(nc.sbuf_tensor("DTS", [P, 2048], BF16))
        DS4 = e(nc.sbuf_tensor("DS4", [P, 128], F32))
        DSX = e(nc.sbuf_tensor("DSX", [P, 1024], BF16))
        DW = e(nc.sbuf_tensor("DW", [P, 2048], BF16))
        DWT = e(nc.sbuf_tensor("DWT", [P, 2048], BF16))

        def mksem(nm):
            return Sem(e(nc.semaphore(nm)), nm)

        def mksem2(nm):
            return [mksem(nm + "0"), mksem(nm + "1")]

        sV = mksem("sV")
        sA = mksem("sA")
        sT = mksem("sT")
        sAG = mksem("sAG")
        sCONST = mksem("sCONST")
        sWKV = mksem("sWKV")
        sWQ = mksem2("sWQ")
        sXL = mksem2("sXL")
        sQS = mksem2("sQS")
        sKSW = mksem("sKSW")
        sQH = mksem2("sQH")
        sPK = mksem2("sPK")
        sSC = mksem2("sSC")
        sDW = mksem2("sDW")
        sDWA = mksem2("sDWA")
        sXC = mksem("sXC")
        sWO = mksem2("sWO")
        sO1 = mksem2("sO1")
        sXR = mksem2("sXR")
        sWG = mksem2("sWG")
        sWD = mksem2("sWD")
        sAC = mksem2("sAC")

        kfm = A5
        vtm = A7
        EB = A6
        A4I = A4[:].bitcast(I32)
        DCC = A4I[:, 0:1024]
        DB1 = A4I[:, 1024:2048]
        DB2 = A4I[:, 2048:3072]
        DB3 = A4I[:, 3072:4096]

        def lnx(j):
            return A3[:, (j % 2) * 8192:(j % 2) * 8192 + 8192]

        cosq = A2[:, 0:1024]
        sinq = A2[:, 1024:2048]

        def qevsl(hd):
            k = hd % 3
            return QEV[:, k * 1024:k * 1024 + 1024] if k < 2 \
                else A2[:, 2048:3072]

        def rtsl(hd):
            k = hd % 3
            return RT[:, k * 1024:k * 1024 + 1024] if k < 2 \
                else A2[:, 3072:4096]
        cosk = A2[:, 4096:6144]
        sink = A2[:, 6144:8192]

        def dsub(i):
            return "gud"[i // 32], (i % 32) // 2, (i % 32) % 2

        def dec_load(g, i):
            if i < 0 or i >= NSUB:
                return
            m, rt, hf = dsub(i)
            ms.wait(g, sV, f"dv_{i-2}")
            sl = (i % 2) * 512
            inst = g.dma_start(out=DPK[:, sl:sl + 512],
                               in_=pks[m][rt * P:(rt + 1) * P, hf * 512:(hf + 1) * 512])
            ms.inc(inst, sPK[i % 2], 16, name=f"dl_{i}")
            if hf == 0:
                ssl = (rt % 2) * 64
                ms.wait(g, sV, f"dv_{i-3}")
                inst = g.dma_start(out=DS4[:, ssl:ssl + 64],
                                   in_=scs[m][rt * P:(rt + 1) * P, :])
                ms.inc(inst, sSC[rt % 2], 16, name=f"scl_{i}")

        def dec_v(v, i):
            if i < 0 or i >= NSUB:
                return
            m, rt, hf = dsub(i)
            ms.wait(v, sPK[i % 2], f"dl_{i}")
            ms.wait(v, sSC[rt % 2], f"scl_{i - hf}")
            pk = DPK[:, (i % 2) * 512:(i % 2) * 512 + 512]
            Cc = DCC
            v.tensor_single_scalar(DCC[:, 0:1024:2], pk, 15, ALU.bitwise_and)
            v.tensor_single_scalar(DCC[:, 1:1024:2], pk, 4, ALU.logical_shift_right)
            v.tensor_single_scalar(DB1, Cc, 7, ALU.bitwise_and)
            v.tensor_single_scalar(DB2, DB1, 1, ALU.logical_shift_right)
            v.tensor_scalar(DB3, Cc, 1, 2, op0=ALU.bitwise_and, op1=ALU.bitwise_or)
            v.tensor_tensor(DB3, DB3, DB2, ALU.logical_shift_left)
            v.tensor_scalar(DB1, DB1, 2, -2, op0=ALU.min, op1=ALU.add)
            v.tensor_tensor(DB3, DB3, DB1, ALU.add)
            v.tensor_single_scalar(DB2, Cc, 3, ALU.logical_shift_right)
            v.tensor_scalar(DB2, DB2, -2, 1, op0=ALU.mult, op1=ALU.add)
            tss = DTS[:, (i % 2) * 1024:(i % 2) * 1024 + 1024]
            v.tensor_tensor(tss, DB3, DB2, ALU.mult)
            ssl = (rt % 2) * 64
            sc4 = DS4[:, ssl + hf * 32: ssl + hf * 32 + 32]
            v.tensor_single_scalar(DSX[:].rearrange("p (a b) -> p a b", b=32),
                                   sc4.unsqueeze(2).to_broadcast((P, 32, 32)),
                                   0.25, ALU.mult)
            ms.wait(v, sT, f"dt_{i-2}")
            dws = DW[:, (i % 2) * 1024:(i % 2) * 1024 + 1024]
            inst = v.tensor_tensor(dws, tss, DSX[:], ALU.mult)
            ms.inc(inst, sV, 1, name=f"dv_{i}")

        def dec_t(t, pdec, i):
            if i < 0 or i >= NSUB:
                return
            ms.wait(t, sV, f"dv_{i}")
            ms.wait(t, sV, f"de_{i-2}")
            dws = DW[:, (i % 2) * 1024:(i % 2) * 1024 + 1024]
            pb = pdec[i % 2]
            last = None
            for j in range(8):
                last = t.transpose(pb[:, j * P:(j + 1) * P], dws[:, j * P:(j + 1) * P],
                                   s_id[:])
            ms.inc(last, sT, 1, name=f"dt_{i}")

        def dec_e(v, pdec, i):
            if i < 0 or i >= NSUB:
                return
            ms.wait(v, sT, f"dt_{i}")
            if i >= 2:
                ms.wait(v, sDW[i % 2], f"dw_{i-2}")
            inst = v.tensor_copy(DWT[:, (i % 2) * 1024:(i % 2) * 1024 + 1024],
                                 pdec[i % 2][:])
            ms.inc(inst, sV, 1, name=f"de_{i}")

        def dec_w(g, i):
            if i < 0 or i >= NSUB:
                return
            m, rt, hf = dsub(i)
            ms.wait(g, sV, f"de_{i}")
            src = DWT[:, (i % 2) * 1024:(i % 2) * 1024 + 1024]
            if m in ("g", "u"):
                dst = agi[m].ap().rearrange("a (p f) -> a p f", p=P)[
                    rt, :, hf * 1024:hf * 1024 + 1024]
                ms.inc(g.dma_start(out=dst, in_=src), sDW[i % 2], 16,
                       name=f"dw_{i}")
            else:
                adh = agi_dh[rt // 8].ap()
                for j in range(8):
                    piece = src[:, j * P:(j + 1) * P]
                    dst = adh[hf * 1024 + j * P:hf * 1024 + (j + 1) * P,
                              (rt % 8) * P:(rt % 8 + 1) * P]
                    nm = f"dw_{i}" if j == 7 else None
                    ms.inc(g.dma_start(out=dst, in_=piece), sDW[i % 2], 16,
                           name=nm)

        def ag_trig(g, m2):
            iN = {"g": 31, "u": 63, "d1": 79, "d2": 95}[m2]
            ms.wait(g, sDW[iN % 2], f"dw_{iN}")
            if m2 in ("d1", "d2"):
                src_, dst_ = agi_dh[0 if m2 == "d1" else 1], \
                    ago_dh[0 if m2 == "d1" else 1]
            else:
                src_, dst_ = agi[m2], ago[m2]
            inst = g.collective_compute(
                "AllGather", ALU.bypass,
                replica_groups=[[0, 1, 2, 3, 4, 5, 6, 7]],
                ins=[src_.ap().opt()], outs=[dst_.ap().opt()])
            ms.inc(inst, sAG, 1, name=f"ag_{m2}")

        def pump_g(g, p):
            dec_load(g, p)
            dec_w(g, p - 5)
            if p - 5 == 31:
                ag_trig(g, "g")

        def pump_v(v, pdec, p):
            dec_v(v, p - 2)
            dec_e(v, pdec, p - 4)

        def pump_t(t, pdec, p):
            dec_t(t, pdec, p - 3)

        with nc.Block() as blk:
            @blk.sync
            def _(s):
                for src, dst in ((identd, s_id[:]), (onescd, s_oc[:]),
                                 (onesrfd, s_orf[:]), (epsd, EPSB[:])):
                    ms.inc(s.dma_start(out=dst, in_=src), sCONST, 16)
                ms.inc(s.dma_start(out=A2[:, 0:2048], in_=tblq), sCONST, 16)
                ms.inc(s.dma_start(out=A2[:, 4096:8192], in_=tblk), sCONST, 16)
                ms.mark(sCONST, "const")

            @blk.gpsimd
            def _(g):
                src = wk.rearrange("(ch p) n -> p ch n", p=P)
                ms.inc(g.dma_start(out=A4[:].rearrange("p (a b) -> p a b", b=KC),
                                   in_=src), sWKV, 16)
                src = wv.rearrange("(ch p) n -> p ch n", p=P)
                ms.inc(g.dma_start(out=A6[:].rearrange("p (a b) -> p a b", b=KC),
                                   in_=src), sWKV, 16)
                ms.mark(sWKV, "wkv")

        ps1 = contextlib.ExitStack()
        pT = [ps1.enter_context(nc.psum_tensor(f"pT{i}", [P, 1024], BF16))
              for i in range(2)]
        pQ = [ps1.enter_context(nc.psum_tensor(f"pQ{i}", [P, 512], F32))
              for i in range(2)]
        pKV = [ps1.enter_context(nc.psum_tensor(f"pKV{i}", [P, 512], F32))
               for i in range(2)]
        pR = [ps1.enter_context(nc.psum_tensor(f"pR{i}", [1, 512], F32))
              for i in range(2)]

        def vsync(v, inst):
            c = ms.inc(inst, sV, 1)
            v.wait_ge(sV.h, c)

        def asyncf(a, inst):
            c = ms.inc(inst, sA, 1)
            a.wait_ge(sA.h, c)

        def ln_stats_v(v, xs, tag):
            vsync(v, v.reduce_sum(STA[:, 0:1], xs, axis=AX))
            vsync(v, v.tensor_single_scalar(STA[:, 1:2], STA[:, 0:1], 1.0 / C,
                                            ALU.mult))
            vsync(v, v.tensor_tensor(STA[:, 8:9], STA[:, 1:2], STA[:, 1:2], ALU.mult))
            inst = v.tensor_scalar(STA[:, 2:3], STA[:, 8:9], -1.0, 1e-5,
                                   op0=ALU.mult, op1=ALU.add)
            ms.inc(inst, sV, 1, name=f"lm_{tag}")

        def ln_stats_a(a, xs, tag):
            ms.wait(a, sV, f"lm_{tag}")
            asyncf(a, a.activation(SQB[:], xs, AF.Square, accum_out=STA[:, 3:4]))
            asyncf(a, a.activation(STA[:, 7:8], STA[:, 3:4], AF.Ln, bias=STA[:, 2:3],
                                   scale=1.0 / C))
            inst = a.activation(STA[:, 5:6], STA[:, 7:8], AF.Exp, scale=-0.5)
            ms.inc(inst, sA, 1, name=f"lr_{tag}")

        def lnx_evac(v, i):
            j = i // 4
            tt = i % 4
            ms.wait(v, sT, f"lt_{i}")
            dst = lnx(j).rearrange("p (a b) -> p a b", b=512)
            last = None
            for gi in range(2):
                src = pT[gi][:].rearrange("p (a b) -> p a b", b=P)
                d3 = dst[:, gi * 8:gi * 8 + 8, tt * P:tt * P + P]
                last = v.tensor_copy(d3, src)
            ms.inc(last, sV, 1, name=f"le_{i}")

        with nc.Block() as blk:
            @blk.gpsimd
            def _(g):
                for qc in range(16 if KCUT >= 3 else 0):
                    if qc >= 2:
                        ms.wait(g, sT, f"qmm_{qc-2}_1")
                    srcw = wq[:, qc * P:(qc + 1) * P].rearrange("(ch p) n -> p ch n",
                                                                p=P)
                    dst = WQB[:, (qc % 2) * 2048:(qc % 2) * 2048 + 2048]
                    inst = g.dma_start(out=dst.rearrange("p (a b) -> p a b", b=P),
                                       in_=srcw)
                    ms.inc(inst, sWQ[qc % 2], 16, name=f"wq_{qc}")

            @blk.sync
            def _(s):
                def xld(i):
                    ms.wait(s, sV, f"lh_x{i-2}")
                    src = (xo if i < 8 else xz)[(i % 8) * P:(i % 8) * P + P, :]
                    inst = s.dma_start(out=XT[:, (i % 2) * C:(i % 2) * C + C], in_=src)
                    ms.inc(inst, sXL[i % 2], 16, name=f"xld_{i}")

                for i in range(10):
                    xld(i)
                for qc in range(16 if KCUT >= 3 else 0):
                    ms.wait(s, sV, f"qev_{qc}_1")
                    src = QEV[:, (qc % 2) * 1024:(qc % 2) * 1024 + 1024]
                    inst = s.dma_start(out=q_sp.ap()[qc * P:(qc + 1) * P, :], in_=src)
                    ms.inc(inst, sQS[qc % 2], 16, name=f"qsp_{qc}")
                for i in range(10, 16):
                    xld(i)
                for kc in range(4 if KCUT >= 4 else 0):
                    ms.wait(s, sV, f"krn_{kc}")
                    ms.inc(s.dma_start(out=SQB[0:64, :], in_=RT[64:128, 0:2048]), sKSW, 16)
                    inst = s.dma_start(out=SQB[64:128, :], in_=RT[0:64, 0:2048])
                    ms.inc(inst, sKSW, 16, name=f"ksw_{kc}")

            @blk.vector
            def _(v):
                ms.wait(v, sCONST, "const")

                def ln_tile(i):
                    xs = XT[:, (i % 2) * C:(i % 2) * C + C]
                    ms.wait(v, sXL[i % 2], f"xld_{i}")
                    ln_stats_v(v, xs, f"x{i}")
                    ms.wait(v, sA, f"lr_x{i}")
                    ms.wait(v, sT, f"lt_{i-1}")
                    inst = v.tensor_scalar(H2[:], xs, STA[:, 1:2], STA[:, 5:6],
                                           op0=ALU.subtract, op1=ALU.mult)
                    ms.inc(inst, sV, 1, name=f"lh_x{i}")
                    if i >= 1 and f"le_{i-1}" not in ms.out:
                        lnx_evac(v, i - 1)

                def kv_evac(j):
                    for kc in range(4):
                        ms.wait(v, sT, f"kmm_{j}_{kc}")
                        inst = v.tensor_copy(
                            kfm[:, kc * 2048 + j * 512:kc * 2048 + j * 512 + 512],
                            pKV[kc % 2][:])
                        ms.inc(inst, sV, 1, name=f"kev_{j}_{kc}")
                    for t4 in range(4):
                        ms.wait(v, sT, f"vmm_{j}_{t4}")
                        inst = v.tensor_copy(
                            vtm[:, (j * 4 + t4) * 512:(j * 4 + t4) * 512 + 512],
                            pKV[t4 % 2][:])
                        ms.inc(inst, sV, 1, name=f"vev_{j}_{t4}")

                for i in range(8):
                    ln_tile(i)
                lnx_evac(v, 7)
                if KCUT >= 2:
                    kv_evac(0)
                    kv_evac(1)
                for qc in range(16 if KCUT >= 3 else 0):
                    for jb in range(2):
                        ms.wait(v, sT, f"qmm_{qc}_{jb}")
                        if qc >= 2 and jb == 0:
                            ms.wait(v, sQS[qc % 2], f"qsp_{qc-2}")
                            ms.wait(v, sA, f"qsq_{qc-2}")
                        dst = QEV[:, (qc % 2) * 1024 + jb * 512:
                                  (qc % 2) * 1024 + jb * 512 + 512]
                        inst = v.tensor_copy(dst, pQ[jb][:])
                        nm = f"qev_{qc}_{jb}" if jb == 1 else None
                        ms.inc(inst, sV, 1, name=nm)
                for i in range(8, 12):
                    ln_tile(i)
                lnx_evac(v, 11)
                if KCUT >= 2:
                    kv_evac(2)
                for i in range(12, 16):
                    ln_tile(i)
                lnx_evac(v, 15)
                if KCUT >= 2:
                    kv_evac(3)
                if KCUT < 4:
                    return
                for half in range(2):
                    ms.wait(v, sT, f"qbc_{half}")
                    inst = v.tensor_copy(QBC[:, half * 512:half * 512 + 512], pQ[0][:])
                    ms.inc(inst, sV, 1, name=f"qbcc_{half}")
                for sh in range(4):
                    ms.wait(v, sT, f"kbc_{sh}")
                    inst = v.tensor_copy(XT[:, sh * 512:sh * 512 + 512], pQ[1][:])
                    ms.inc(inst, sV, 1, name=f"kbcc_{sh}")
                for kc in range(4):
                    kch = kfm[:, kc * 2048:kc * 2048 + 2048]
                    inst = v.tensor_tensor(RT[:, 0:2048], kch, XT[:, 0:2048], ALU.mult)
                    ms.inc(inst, sV, 1, name=f"krn_{kc}")
                    v.tensor_tensor(H2[:], RT[:, 0:2048], cosk, ALU.mult)
                    ms.wait(v, sKSW, f"ksw_{kc}")
                    v.tensor_tensor(RT[:, 0:2048], SQB[:], sink, ALU.mult)
                    inst = v.tensor_tensor(kch, H2[:], RT[:, 0:2048], ALU.add)
                    ms.inc(inst, sV, 1, name=f"krope_{kc}")
                ms.mark(sV, "krope")

            @blk.scalar
            def _(a):
                for i in range(8):
                    xs = XT[:, (i % 2) * C:(i % 2) * C + C]
                    ln_stats_a(a, xs, f"x{i}")
                for qc in range(16 if KCUT >= 3 else 0):
                    ms.wait(a, sV, f"qev_{qc}_1")
                    if qc >= 1:
                        ms.wait(a, sT, f"qsum_{qc-1}")
                    inst = a.activation(SQB[:, 0:1024],
                                        QEV[:, (qc % 2) * 1024:(qc % 2) * 1024 + 1024],
                                        AF.Square)
                    ms.inc(inst, sA, 1, name=f"qsq_{qc}")
                if KCUT >= 3:
                    ms.wait(a, sT, "qsumF")
                for i in range(8, 16):
                    xs = XT[:, (i % 2) * C:(i % 2) * C + C]
                    ln_stats_a(a, xs, f"x{i}")
                if KCUT < 4:
                    return
                for half in range(2):
                    ms.wait(a, sT, "qsumF")
                    asyncf(a, a.activation(ROWS[0:1, 2048:2560],
                                           pR[half][0:1, :], AF.Ln,
                                           bias=EPSB[0:1, 0:1], scale=1.0 / C))
                    inst = a.activation(ROWS[0:1, half * 512:half * 512 + 512],
                                        ROWS[0:1, 2048:2560],
                                        AF.Exp, scale=-0.5)
                    ms.inc(inst, sA, 1, name=f"qrow_{half}")
                for ph in range(2):
                    for kc in range(4):
                        ms.wait(a, sV, f"kev_3_{kc}")
                        if not (ph == 0 and kc == 0):
                            pv = f"ksum_{ph}_{kc-1}" if kc >= 1 else "ksumF_0"
                            ms.wait(a, sT, pv)
                        inst = a.activation(SQB[:, 0:2048],
                                            kfm[:, kc * 2048:kc * 2048 + 2048],
                                            AF.Square)
                        ms.inc(inst, sA, 1, name=f"ksq_{ph}_{kc}")
                    for sh2 in range(2):
                        sh = ph * 2 + sh2
                        ms.wait(a, sT, f"ksumF_{ph}")
                        r = ROWS[0:1, 1024 + sh * 512:1024 + sh * 512 + 512]
                        a.activation(r, pR[sh2][0:1, :], AF.Ln,
                                     bias=EPSB[0:1, 0:1], scale=1.0 / KC)
                        inst = a.activation(r, r, AF.Exp, scale=-0.5)
                        ms.inc(inst, sA, 1, name=f"krow_{sh}")

            @blk.tensor
            def _(t):
                ms.wait(t, sCONST, "const")
                ms.wait(t, sWKV, "wkv")

                def tile_trans(i):
                    ms.wait(t, sV, f"lh_x{i}")
                    ms.wait(t, sV, f"le_{i-1}")
                    last = None
                    for gi in range(2):
                        for j in range(8):
                            cch = gi * 8 + j
                            last = t.transpose(pT[gi][:, j * P:(j + 1) * P],
                                               H2[:, cch * P:(cch + 1) * P], s_id[:])
                    ms.inc(last, sT, 1, name=f"lt_{i}")

                def kv_block(j):
                    for kc in range(4):
                        if kc >= 2:
                            ms.wait(t, sV, f"kev_{j}_{kc-2}")
                        if j >= 1:
                            ms.wait(t, sV, f"kev_{j-1}_{kc}")
                            ms.wait(t, sV, f"vev_{j-1}_{2 + (kc % 2)}")
                        lastm = None
                        for cch in range(16):
                            lastm = t.matmul(
                                pKV[kc % 2][:],
                                A4[:, cch * 512 + kc * 128:cch * 512 + kc * 128 + 128],
                                lnx(j)[:, cch * 512:cch * 512 + 512],
                                start=(cch == 0), stop=(cch == 15))
                        ms.inc(lastm, sT, 1, name=f"kmm_{j}_{kc}")
                    for t4 in range(4):
                        if t4 >= 2:
                            ms.wait(t, sV, f"vev_{j}_{t4-2}")
                        if j >= 1:
                            ms.wait(t, sV, f"vev_{j-1}_{2 + (t4 % 2)}")
                        ms.wait(t, sV, f"kev_{j}_{2 + (t4 % 2)}")
                        lastm = None
                        for cch in range(16):
                            lastm = t.matmul(
                                pKV[t4 % 2][:],
                                lnx(j)[:, cch * 512 + t4 * 128:cch * 512 + t4 * 128 + 128],
                                A6[:, cch * 512:cch * 512 + 512],
                                start=(cch == 0), stop=(cch == 15))
                        ms.inc(lastm, sT, 1, name=f"vmm_{j}_{t4}")

                for i in range(8):
                    tile_trans(i)
                for j in (0, 1):
                    ms.wait(t, sV, f"le_{j * 4 + 3}")
                    kv_block(j)
                def qsum_mm(qs):
                    ms.wait(t, sA, f"qsq_{qs}")
                    lastm = None
                    for half in range(2):
                        lastm = t.matmul(pR[half][0:1, :], s_oc[:, 0:1],
                                         SQB[:, half * 512:half * 512 + 512],
                                         start=(qs == 0), stop=(qs == 15))
                    nm = "qsumF" if qs == 15 else f"qsum_{qs}"
                    ms.inc(lastm, sT, 1, name=nm)

                for qc in range(16 if KCUT >= 3 else 0):
                    ms.wait(t, sWQ[qc % 2], f"wq_{qc}")
                    if qc >= 1:
                        ms.wait(t, sV, f"qev_{qc-1}_1")
                    lastm = None
                    for cch in range(16):
                        for jb in range(2):
                            lastm = t.matmul(
                                pQ[jb][:],
                                WQB[:, (qc % 2) * 2048 + cch * 128:
                                    (qc % 2) * 2048 + cch * 128 + 128],
                                lnx(jb)[:, cch * 512:cch * 512 + 512],
                                start=(cch == 0), stop=(cch == 15))
                    ms.inc(lastm, sT, 1, name=f"qmm_{qc}_1")
                    ms.mark(sT, f"qmm_{qc}_0")
                    if qc >= 1:
                        qsum_mm(qc - 1)
                if KCUT >= 3:
                    qsum_mm(15)
                for i in range(8, 12):
                    tile_trans(i)
                if KCUT >= 2:
                    ms.wait(t, sV, "le_11")
                    kv_block(2)
                for i in range(12, 16):
                    tile_trans(i)
                if KCUT >= 2:
                    ms.wait(t, sV, "le_15")
                    kv_block(3)
                if KCUT < 4:
                    return
                for half in range(2):
                    ms.wait(t, sA, f"qrow_{half}")
                    if half >= 1:
                        ms.wait(t, sV, "qbcc_0")
                    rows = ROWS[0:1, half * 512:half * 512 + 512]
                    lastm = t.matmul(pQ[0][:], s_orf[0:1, :], rows,
                                     start=True, stop=True)
                    ms.inc(lastm, sT, 1, name=f"qbc_{half}")
                for ph in range(2):
                    for kc in range(4):
                        ms.wait(t, sA, f"ksq_{ph}_{kc}")
                        lastm = None
                        for sh2 in range(2):
                            sh = ph * 2 + sh2
                            lastm = t.matmul(pR[sh2][0:1, :], s_oc[:, 0:1],
                                             SQB[:, sh * 512:sh * 512 + 512],
                                             start=(kc == 0), stop=(kc == 3))
                        nm = f"ksumF_{ph}" if kc == 3 else f"ksum_{ph}_{kc}"
                        ms.inc(lastm, sT, 1, name=nm)
                    for sh2 in range(2):
                        sh = ph * 2 + sh2
                        ms.wait(t, sA, f"krow_{sh}")
                        if sh >= 1:
                            ms.wait(t, sV, f"kbcc_{sh-1}")
                        rows = ROWS[0:1, 1024 + sh * 512:1024 + sh * 512 + 512]
                        lastm = t.matmul(pQ[1][:], s_orf[0:1, :], rows,
                                         start=True, stop=True)
                        ms.inc(lastm, sT, 1, name=f"kbc_{sh}")

        ps1.close()

        if KLV < 2:
            return nc
        psD = contextlib.ExitStack()
        pDEC = [psD.enter_context(nc.psum_tensor(f"pDEC{i}", [P, 1024], BF16))
                for i in range(2)]

        ps2 = contextlib.ExitStack()
        pL = [ps2.enter_context(nc.psum_tensor(f"pL{i}", [P, 512], F32))
              for i in range(2)]
        pS = ps2.enter_context(nc.psum_tensor("pS", [1, 512], F32))
        pE = [ps2.enter_context(nc.psum_tensor(f"pE{i}", [P, 512], F32))
              for i in range(2)]
        pB = ps2.enter_context(nc.psum_tensor("pB", [P, 512], F32))

        with nc.Block() as blk:
            @blk.sync
            def _(s):
                for hd in range(16):
                    if hd >= 3:
                        ms.wait(s, sT, f"lg_{(hd - 3) * 2 + 1}_15")
                        if KDBG:
                            ms.wait(s, sCONST, f"dbq_{hd-3}")
                    else:
                        ms.wait(s, sQS[1], "qsp_15")
                        ms.wait(s, sA, "qsq_15")
                        ms.wait(s, sV, "krope")
                    inst = s.dma_start(out=qevsl(hd),
                                       in_=q_sp.ap()[hd * P:(hd + 1) * P, :])
                    ms.inc(inst, sQH[hd % 2], 16)
                    qsw = rtsl(hd)
                    ms.inc(s.dma_start(out=qsw[0:64, :],
                                       in_=q_sp.ap()[hd * P + 64:hd * P + 128, :]),
                           sQH[hd % 2], 16)
                    inst = s.dma_start(out=qsw[64:128, :],
                                       in_=q_sp.ap()[hd * P:hd * P + 64, :])
                    ms.inc(inst, sQH[hd % 2], 16, name=f"qh_{hd}")
                    if KDBG:
                        ms.wait(s, sV, f"qp_{hd}")
                        inst = s.dma_start(
                            out=d_qp[hd * P:(hd + 1) * P, :],
                            in_=qevsl(hd))
                        ms.inc(inst, sCONST, 16, name=f"dbq_{hd}")
                if KDBG:
                    ms.wait(s, sV, "krope")
                    ms.inc(s.dma_start(out=d_k, in_=kfm[:, 0:8192]), sCONST, 16)
                    ms.inc(s.dma_start(out=d_v, in_=vtm[:, 0:8192]), sCONST, 16)
                    ms.inc(s.dma_start(out=d_q, in_=q_sp.ap()), sCONST, 16)
                    ms.wait(s, sA, "ex_0_15")
                    ms.inc(s.dma_start(out=d_eb, in_=EB[:, 0:8192]), sCONST, 16)

            @blk.vector
            def _(v):
                def qprep(hd):
                    if hd < 0 or hd >= 16:
                        return
                    ms.wait(v, sQH[hd % 2], f"qh_{hd}")
                    sl = qevsl(hd)
                    qsw = rtsl(hd)
                    v.tensor_tensor(H2[:, 0:1024], sl, cosq, ALU.mult)
                    v.tensor_tensor(SQB[:, 0:1024], qsw, sinq, ALU.mult)
                    v.tensor_tensor(SQB[:, 1024:2048], H2[:, 0:1024], SQB[:, 0:1024],
                                    ALU.add)
                    inst = v.tensor_tensor(sl, SQB[:, 1024:2048], QBC[:], ALU.mult)
                    ms.inc(inst, sV, 1, name=f"qp_{hd}")

                qprep(0)
                qprep(1)
                for it in range(32):
                    hd, tt = it // 2, it % 2
                    if tt == 1:
                        qprep(hd + 2)
                    if it >= 1:
                        pit = it - 1
                        phd = pit // 2
                        ms.wait(v, sA, f"rb_{pit}")
                        dst = A3[:, phd * 1024 + (pit % 2) * 512:
                                 phd * 1024 + (pit % 2) * 512 + 512]
                        inst = v.tensor_tensor(
                            dst, pE[pit % 2][:],
                            RBS[:, (pit % 2) * 512:(pit % 2) * 512 + 512], ALU.mult)
                        ms.inc(inst, sV, 1, name=f"encv_{pit}")
                    pump_v(v, pDEC, it * 2)
                    pump_v(v, pDEC, it * 2 + 1)
                pit = 31
                ms.wait(v, sA, f"rb_{pit}")
                dst = A3[:, 15 * 1024 + 512:15 * 1024 + 1024]
                inst = v.tensor_tensor(dst, pE[pit % 2][:],
                                       RBS[:, (pit % 2) * 512:(pit % 2) * 512 + 512],
                                       ALU.mult)
                ms.inc(inst, sV, 1, name=f"encv_{pit}")

            @blk.scalar
            def _(a):
                def wcv_load(i):
                    if i < 0 or i >= 32:
                        return
                    ch, h = i // 2, i % 2
                    inst = a.dma_start(
                        out=XT[:, (i % 2) * 1024:(i % 2) * 1024 + 1024],
                        in_=wo[ch * P:(ch + 1) * P, h * 1024:(h + 1) * 1024])
                    ms.inc(inst, sO1[0], 16, name=f"wcl_{i}")

                wcv_load(0)
                wcv_load(1)
                for it in range(32):
                    for stt in range(16):
                        ms.wait(a, sT, f"lg_{it}_{stt}")
                        inst = a.activation(EB[:, stt * 512:stt * 512 + 512],
                                            pL[stt % 2][:], AF.Exp)
                        ms.inc(inst, sA, 1, name=f"ex_{it}_{stt}")
                    ms.wait(a, sT, f"ss_{it}")
                    asyncf(a, a.activation(ROWS[0:1, 512:1024], pS[0:1, :], AF.Ln))
                    inst = a.activation(ROWS[0:1, 0:512], ROWS[0:1, 512:1024],
                                        AF.Exp, scale=-1.0)
                    ms.inc(inst, sA, 1, name=f"rr_{it}")
                    ms.wait(a, sT, f"pb_{it}")
                    inst = a.activation(RBS[:, (it % 2) * 512:(it % 2) * 512 + 512],
                                        pB[:], AF.Copy)
                    ms.inc(inst, sA, 1, name=f"rb_{it}")
                    ms.wait(a, sO1[0], f"wcl_{it}")
                    if it >= 4:
                        ms.wait(a, sO1[1], f"wcs_{it-4}")
                    inst = a.activation(
                        WQB[:, (it % 4) * 1024:(it % 4) * 1024 + 1024],
                        XT[:, (it % 2) * 1024:(it % 2) * 1024 + 1024], AF.Copy)
                    ms.inc(inst, sA, 1, name=f"wcc_{it}")
                    wcv_load(it + 2)

            @blk.tensor
            def _(t):
                for it in range(32):
                    hd, tt = it // 2, it % 2
                    kh = hd // 4
                    if tt == 0:
                        ms.wait(t, sV, f"qp_{hd}")
                    if it >= 2:
                        ms.wait(t, sV, f"encv_{it-2}")
                    qsl = qevsl(hd)[:, tt * 512:tt * 512 + 512]
                    for stt in range(16):
                        if stt >= 2:
                            ms.wait(t, sA, f"ex_{it}_{stt-2}")
                        elif it >= 1:
                            ms.wait(t, sA, f"ex_{it-1}_{14 + stt}")
                        lastm = t.matmul(
                            pL[stt % 2][:],
                            kfm[:, kh * 2048 + stt * P:kh * 2048 + stt * P + P],
                            qsl, start=True, stop=True)
                        ms.inc(lastm, sT, 1, name=f"lg_{it}_{stt}")
                        if stt >= 1:
                            ms.wait(t, sA, f"ex_{it}_{stt-1}")
                            eb = EB[:, (stt - 1) * 512:stt * 512]
                            t.matmul(pS[0:1, :], s_oc[:, 0:1], eb,
                                     start=(stt == 1), stop=False)
                            t.matmul(pE[it % 2][:],
                                     vtm[:, (stt - 1) * 512 + kh * P:
                                         (stt - 1) * 512 + kh * P + P],
                                     eb, start=(stt == 1), stop=False)
                    ms.wait(t, sA, f"ex_{it}_15")
                    eb = EB[:, 15 * 512:16 * 512]
                    lastm = t.matmul(pS[0:1, :], s_oc[:, 0:1], eb,
                                     start=False, stop=True)
                    ms.inc(lastm, sT, 1, name=f"ss_{it}")
                    lastm = t.matmul(pE[it % 2][:],
                                     vtm[:, 15 * 512 + kh * P:15 * 512 + kh * P + P],
                                     eb, start=False, stop=True)
                    ms.inc(lastm, sT, 1, name=f"pv_{it}")
                    ms.wait(t, sA, f"rr_{it}")
                    if it >= 1:
                        ms.wait(t, sA, f"rb_{it-1}")
                    lastm = t.matmul(pB[:], s_orf[0:1, :], ROWS[0:1, 0:512],
                                     start=True, stop=True)
                    ms.inc(lastm, sT, 1, name=f"pb_{it}")
                    pump_t(t, pDEC, it * 2)
                    pump_t(t, pDEC, it * 2 + 1)

            @blk.gpsimd
            def _(g):
                def wcv_store(i):
                    if i < 0 or i >= 32:
                        return
                    ch, h = i // 2, i % 2
                    ms.wait(g, sA, f"wcc_{i}")
                    dstv = wo_bf.ap().rearrange("p (q x) -> p q x", q=4)[
                        :, 2 * h:2 * h + 2, ch * 512:(ch + 1) * 512]
                    inst = g.dma_start(
                        out=dstv,
                        in_=WQB[:, (i % 4) * 1024:(i % 4) * 1024 + 1024
                                ].rearrange("p (a b) -> p a b", b=512))
                    ms.inc(inst, sO1[1], 16, name=f"wcs_{i}")

                for it in range(32):
                    pump_g(g, it * 2)
                    pump_g(g, it * 2 + 1)
                    wcv_store(it - 4)
                for i in range(28, 32):
                    wcv_store(i)
                ms.mark(sO1[1], "wocvD")

        ps2.close()
        if KLV < 3:
            return nc

        ps3 = contextlib.ExitStack()
        pW = [ps3.enter_context(nc.psum_tensor(f"pW{i}", [P, 512], F32))
              for i in range(4)]
        pT3 = [ps3.enter_context(nc.psum_tensor(f"pT3{i}", [P, 1024], BF16))
               for i in range(2)]

        def wsl3(i):
            return A5 if i % 2 == 0 else A6


        def ht3(t, tt8):
            if tt8 < 0 or tt8 > 7:
                return
            ms.wait(t, sV, f"lh2_{tt8}")
            if tt8 >= 1:
                ms.wait(t, sV, f"he_{tt8-1}")
            last = None
            for gi in range(2):
                for j in range(8):
                    cch = gi * 8 + j
                    last = t.transpose(pT3[gi][:, j * P:(j + 1) * P],
                                       H2[:, cch * P:(cch + 1) * P], s_id[:])
            ms.inc(last, sT, 1, name=f"ht_{tt8}")

        def he3(v, tt8):
            if tt8 < 0 or tt8 > 7:
                return
            ms.wait(v, sT, f"ht_{tt8}")
            dst = A2[:].rearrange("p (a b) -> p a b", b=1024)
            last = None
            for gi in range(2):
                srcp = pT3[gi][:].rearrange("p (a b) -> p a b", b=P)
                d3 = dst[:, gi * 8:gi * 8 + 8, tt8 * P:tt8 * P + P]
                last = v.tensor_copy(d3, srcp)
            ms.inc(last, sV, 1, name=f"he_{tt8}")

        with nc.Block() as blk:
            @blk.sync
            def _(s):
                for jp in range(4):
                    for t2 in range(2):
                        tt8 = 2 * jp + t2
                        if jp == 0 and t2 == 0:
                            ms.wait(s, sO1[1], "wocvD")
                        if jp >= 1:
                            ms.wait(s, sV, f"lh2_{tt8-2}")
                        inst = s.dma_start(
                            out=XT[:, t2 * 2048:(t2 + 1) * 2048],
                            in_=xo[tt8 * P:(tt8 + 1) * P, :])
                        ms.inc(inst, sXC, 16, name=f"xl_{tt8}")
                    for q in range(4):
                        i = jp * 4 + q
                        if i >= 2:
                            ms.wait(s, sT, f"wom_{i-2}_1")
                        inst = s.dma_start(
                            out=wsl3(i)[:],
                            in_=wo_bf.ap()[:, q * 8192:(q + 1) * 8192])
                        ms.inc(inst, sWO[i % 2], 16, name=f"wol_{i}")
                    for t2 in range(2):
                        tt8 = 2 * jp + t2
                        ms.wait(s, sV, f"oev_{tt8}_3")
                        inst = s.dma_start(out=out[tt8 * P:(tt8 + 1) * P, :],
                                           in_=XT[:, t2 * 2048:(t2 + 1) * 2048])
                        ms.inc(inst, sXR[t2], 16, name=f"xw_{tt8}")
                ms.mark(sXR[0], "xwD0")
                ms.mark(sXR[1], "xwD1")
                if KDBG:
                    ms.wait(s, sAG, "ag_g")
                    ms.inc(s.dma_start(out=d_wg, in_=ago["g"].ap()), sCONST, 16)
                    ms.wait(s, sV, "encv_31")
                    ms.inc(s.dma_start(out=d_enc, in_=A3[:, 0:16384]), sCONST, 16)
                    ms.wait(s, sV, "he_7")
                    ms.inc(s.dma_start(out=d_h, in_=A2[:, 0:16384]), sCONST, 16)

            @blk.tensor
            def _(t):
                for jp in range(4):
                    for q in range(4):
                        i = jp * 4 + q
                        if q == 0:
                            ht3(t, 2 * jp - 2)
                        if q == 2:
                            ht3(t, 2 * jp - 1)
                        ms.wait(t, sWO[i % 2], f"wol_{i}")
                        for t2 in range(2):
                            tt8 = 2 * jp + t2
                            k = i * 2 + t2
                            pb = pW[k % 4]
                            if k >= 4:
                                ip = i - 2
                                ms.wait(t, sV,
                                        f"oev_{2 * (ip // 4) + t2}_{ip % 4}")
                            for ch in range(16):
                                t.matmul(pb[:, 0:512],
                                         A3[:, ch * 1024 + tt8 * P:
                                            ch * 1024 + tt8 * P + P],
                                         wsl3(i)[:, ch * 512:(ch + 1) * 512],
                                         start=(ch == 0), stop=False)
                            ms.wait(t, sXC, f"xl_{tt8}")
                            if i == 0 and t2 == 0:
                                ms.wait(t, sA, "idf3")
                            lastm = t.matmul(pb[:, 0:512], RBS[:, 0:128].bitcast(F32),
                                             XT[:, t2 * 2048 + q * 512:
                                                t2 * 2048 + q * 512 + 512],
                                             start=False, stop=True)
                            ms.inc(lastm, sT, 1, name=f"wom_{i}_{t2}")
                        pump_t(t, pDEC, 64 + i * 2)
                        pump_t(t, pDEC, 64 + i * 2 + 1)
                        if i == 15:
                            pump_t(t, pDEC, 96)
                            pump_t(t, pDEC, 97)
                ht3(t, 6)
                ht3(t, 7)

            @blk.vector
            def _(v):
                for jp in range(4):
                    for q in range(4):
                        i = jp * 4 + q
                        for t2 in range(2):
                            tt8 = 2 * jp + t2
                            k = i * 2 + t2
                            ms.wait(v, sT, f"wom_{i}_{t2}")
                            inst = v.tensor_copy(
                                XT[:, t2 * 2048 + q * 512:
                                   t2 * 2048 + q * 512 + 512],
                                pW[k % 4][:, 0:512])
                            ms.inc(inst, sV, 1, name=f"oev_{tt8}_{q}")
                        pump_v(v, pDEC, 64 + i * 2)
                        pump_v(v, pDEC, 64 + i * 2 + 1)
                        if i == 15:
                            pump_v(v, pDEC, 96)
                            pump_v(v, pDEC, 97)
                        if q == 0:
                            he3(v, 2 * jp - 2)
                        if q == 2:
                            he3(v, 2 * jp - 1)
                    for t2 in range(2):
                        tt8 = 2 * jp + t2
                        xr = XT[:, t2 * 2048:(t2 + 1) * 2048]
                        ln_stats_v(v, xr, f"r{tt8}")
                        ms.wait(v, sA, f"lr_r{tt8}")
                        ms.wait(v, sT, f"ht_{tt8-1}")
                        inst = v.tensor_scalar(H2[:], xr, STA[:, 1:2],
                                               STA[:, 5:6],
                                               op0=ALU.subtract, op1=ALU.mult)
                        ms.inc(inst, sV, 1, name=f"lh2_{tt8}")
                he3(v, 6)
                he3(v, 7)

            @blk.scalar
            def _(a):
                inst = a.activation(RBS[:, 0:128].bitcast(F32), s_id[:], AF.Copy)
                ms.inc(inst, sA, 1, name="idf3")
                for tt8 in range(8):
                    xr = XT[:, (tt8 % 2) * 2048:(tt8 % 2) * 2048 + 2048]
                    ln_stats_a(a, xr, f"r{tt8}")

            @blk.gpsimd
            def _(g):
                for i in range(16):
                    pump_g(g, 64 + i * 2)
                    pump_g(g, 64 + i * 2 + 1)
                    if i == 2:
                        ag_trig(g, "u")
                    if i == 10:
                        ag_trig(g, "d1")
                    if i == 15:
                        pump_g(g, 96)
                        pump_g(g, 97)

        ps3.close()
        if KLV < 4:
            return nc

        with nc.Block() as blk:
            @blk.vector
            def _(v):
                for p in range(98, NPUMP):
                    pump_v(v, pDEC, p)

            @blk.tensor
            def _(t):
                for p in range(98, NPUMP):
                    pump_t(t, pDEC, p)

            @blk.gpsimd
            def _(g):
                for p in range(98, NPUMP):
                    pump_g(g, p)
                ag_trig(g, "d2")

        psD.close()
        if KLV < 5:
            return nc

        ps5 = contextlib.ExitStack()
        pG = ps5.enter_context(nc.psum_tensor("pG", [P, 1024], F32))
        pU = ps5.enter_context(nc.psum_tensor("pU", [P, 1024], F32))
        pDn = [ps5.enter_context(nc.psum_tensor(f"pDn{i}", [P, 512], F32))
               for i in range(2)]

        def isl(gc):
            if gc < 8:
                return A5[:, gc * 1024:gc * 1024 + 1024]
            return A6[:, (gc - 8) * 1024:(gc - 8) * 1024 + 1024]

        def accL(b, cc, tt8):
            return b * 32 + cc * 8 + tt8

        def accN(L):
            b, r = divmod(L, 32)
            cc, tt8 = divmod(r, 8)
            return f"acc_{b}_{cc}_{tt8}"

        with nc.Block() as blk:
            if KDBG:
                @blk.sync
                def _(s):
                    ms.wait(s, sAG, "ag_d1")
                    ms.wait(s, sAG, "ag_d2")
                    ms.inc(s.dma_start(out=d_wd[:, 0:1024], in_=ago_dh[0].ap()),
                           sCONST, 16)
                    ms.inc(s.dma_start(out=d_wd[:, 1024:2048], in_=ago_dh[1].ap()),
                           sCONST, 16)

            @blk.sync
            def _(s):
                for g in range(128):
                    if g == 0:
                        ms.wait(s, sAG, "ag_g")
                        ms.wait(s, sAG, "ag_u")
                        ms.wait(s, sV, "dv_95")
                        ms.wait(s, sT, "pv_31")
                    if g >= 2:
                        ms.wait(s, sT, f"um_{g - 2}")
                    rank, rtl = g // 16, g % 16
                    for m, arr, base in (("g", ago["g"], 0), ("u", ago["u"], 4096)):
                        srcw = arr.ap().rearrange("a (p f) -> a p f", p=P)[
                            rank * 16 + rtl, :, :]
                        dst = A4[:, base + (g % 2) * 2048:base + (g % 2) * 2048 + 2048]
                        inst = s.dma_start(out=dst, in_=srcw)
                        ms.inc(inst, sWG[g % 2], 16, name=f"w{m}l_{g}")

            @blk.scalar
            def _(a):
                for b in range(8):
                    for gc in range(16):
                        g = b * 16 + gc
                        ms.wait(a, sT, f"gm_{g}")
                        if g >= 2:
                            g2 = g - 2
                            ms.wait(a, sV, f"int_{g2 // 16}_{g2 % 16}")
                        inst = a.activation(A7[:, (g % 2) * 1024:(g % 2) * 1024 + 1024],
                                            pG[:], AF.Gelu_apprx_tanh)
                        ms.inc(inst, sA, 1, name=f"gg_{g}")
                        ms.wait(a, sT, f"um_{g}")
                        inst = a.activation(A7[:, 2048 + (g % 2) * 1024:
                                               2048 + (g % 2) * 1024 + 1024],
                                            pU[:], AF.Copy)
                        ms.inc(inst, sA, 1, name=f"gu_{g}")

            @blk.tensor
            def _(t):
                for b in range(8):
                    for gc in range(16):
                        g = b * 16 + gc
                        ms.wait(t, sWG[g % 2], f"wgl_{g}")
                        ms.wait(t, sWG[g % 2], f"wul_{g}")
                        if g == 0:
                            ms.wait(t, sV, "he_7")
                        if g >= 1:
                            ms.wait(t, sA, f"gg_{g-1}")
                            ms.wait(t, sA, f"gu_{g-1}")
                        wgs = A4[:, (g % 2) * 2048:(g % 2) * 2048 + 2048]
                        wus = A4[:, 4096 + (g % 2) * 2048:4096 + (g % 2) * 2048 + 2048]
                        lastm = None
                        for cch in range(16):
                            for tt2 in range(2):
                                lastm = t.matmul(
                                    pG[:, tt2 * 512:tt2 * 512 + 512],
                                    wgs[:, cch * 128:cch * 128 + 128],
                                    A2[:, cch * 1024 + tt2 * 512:
                                       cch * 1024 + tt2 * 512 + 512],
                                    start=(cch == 0), stop=(cch == 15))
                        ms.inc(lastm, sT, 1, name=f"gm_{g}")
                        lastm = None
                        for cch in range(16):
                            for tt2 in range(2):
                                lastm = t.matmul(
                                    pU[:, tt2 * 512:tt2 * 512 + 512],
                                    wus[:, cch * 128:cch * 128 + 128],
                                    A2[:, cch * 1024 + tt2 * 512:
                                       cch * 1024 + tt2 * 512 + 512],
                                    start=(cch == 0), stop=(cch == 15))
                        ms.inc(lastm, sT, 1, name=f"um_{g}")
                    ms.wait(t, sV, f"int_{b}_15")
                    for cc in range(4):
                        ms.wait(t, sWD[(b * 4 + cc) % 2], f"wdl_{b}_{cc}")
                        for tt8 in range(8):
                            i = cc * 8 + tt8
                            if i >= 2:
                                pi = i - 2
                                ms.wait(t, sV, f"o2e_{b}_{pi // 8}_{pi % 8}")
                            lastm = None
                            for gc in range(16):
                                lastm = t.matmul(
                                    pDn[i % 2][:],
                                    isl(gc)[:, tt8 * P:tt8 * P + P],
                                    A3[:, ((b * 4 + cc) % 2) * 8192 + gc * 512:
                                       ((b * 4 + cc) % 2) * 8192 + gc * 512 + 512],
                                    start=(gc == 0), stop=(gc == 15))
                            ms.inc(lastm, sT, 1, name=f"dn_{b}_{cc}_{tt8}")

            @blk.vector
            def _(v):
                for b in range(8):
                    for gc in range(16):
                        g = b * 16 + gc
                        ms.wait(v, sA, f"gg_{g}")
                        ms.wait(v, sA, f"gu_{g}")
                        if b >= 1 and gc == 0:
                            ms.wait(v, sT, f"dn_{b-1}_3_7")
                        inst = v.tensor_tensor(
                            isl(gc), A7[:, (g % 2) * 1024:(g % 2) * 1024 + 1024],
                            A7[:, 2048 + (g % 2) * 1024:2048 + (g % 2) * 1024 + 1024],
                            ALU.mult)
                        ms.inc(inst, sV, 1, name=f"int_{b}_{gc}")
                    for cc in range(4):
                        for tt8 in range(8):
                            i = cc * 8 + tt8
                            L = accL(b, cc, tt8)
                            ms.wait(v, sT, f"dn_{b}_{cc}_{tt8}")
                            if L >= 2:
                                ms.wait(v, sAC[L % 2], accN(L - 2))
                            o2 = A7[:, 4096 + (i % 2) * 1024:4096 + (i % 2) * 1024 + 1024]
                            inst = v.tensor_copy(o2.bitcast(F32), pDn[i % 2][:])
                            ms.inc(inst, sV, 1, name=f"o2e_{b}_{cc}_{tt8}")

            @blk.gpsimd
            def _(g):
                def wds_load(g, b, cc):
                    if b > 7:
                        return
                    idx = b * 4 + cc
                    if idx == 0:
                        ms.wait(g, sAG, "ag_d1")
                        ms.wait(g, sT, "wom_15_1")
                    if idx == 2:
                        ms.wait(g, sAG, "ag_d2")
                    if idx >= 2:
                        pcc = idx - 2
                        ms.wait(g, sT, f"dn_{pcc // 4}_{pcc % 4}_7")
                    lastd = None
                    for gc2 in range(16):
                        srcd = ago_dh[cc // 2].ap()[
                            (b * 16 + gc2) * P:(b * 16 + gc2 + 1) * P,
                            (cc % 2) * 512:(cc % 2) * 512 + 512]
                        dst = A3[:, (idx % 2) * 8192 + gc2 * 512:
                                 (idx % 2) * 8192 + gc2 * 512 + 512]
                        lastd = g.dma_start(out=dst, in_=srcd)
                        ms.inc(lastd, sWD[idx % 2], 16)
                    ms.mark(sWD[idx % 2], f"wdl_{b}_{cc}")

                ms.wait(g, sXR[0], "xwD0")
                ms.wait(g, sXR[1], "xwD1")
                wds_load(g, 0, 0)
                wds_load(g, 0, 1)
                for b in range(8):
                    for cc in range(4):
                        for tt8 in range(8):
                            i = cc * 8 + tt8
                            L = accL(b, cc, tt8)
                            ms.wait(g, sV, f"o2e_{b}_{cc}_{tt8}")
                            if b >= 1:
                                ms.wait(g, sAC[L % 2], f"acc_{b-1}_{cc}_{tt8}")
                            o2 = A7[:, 4096 + (i % 2) * 1024:
                                    4096 + (i % 2) * 1024 + 1024]
                            inst = g.dma_start(
                                out=out[tt8 * P:(tt8 + 1) * P, cc * 512:(cc + 1) * 512],
                                in_=o2.bitcast(F32), accum_op=ALU.add)
                            ms.inc(inst, sAC[L % 2], 16, name=f"acc_{b}_{cc}_{tt8}")
                        if cc == 0:
                            wds_load(g, b, 2)
                        elif cc == 1:
                            wds_load(g, b, 3)
                        elif cc == 2:
                            wds_load(g, b + 1, 0)
                        else:
                            wds_load(g, b + 1, 1)
                ms.wait(g, sAC[0], "acc_7_3_6")
                ms.wait(g, sAC[1], "acc_7_3_7")

        ps5.close()
    return nc


_CACHE = {}


def _consts():
    if "c" in _CACHE:
        return _CACHE["c"]
    HH = 128
    fraction = (np.arange(0, HH, 2) / HH).astype(np.float32)
    timescale = (10000.0 ** fraction).astype(np.float32)
    sinus = np.arange(TA, dtype=np.float32)[:, None] * (1.0 / timescale)[None, :]
    sinus = np.concatenate([sinus, sinus], axis=1)
    sin_t = np.sin(sinus).astype(np.float32)
    cos_t = np.cos(sinus).astype(np.float32)
    s = np.float32(1.0 / np.sqrt(HH))
    bf = ml_dtypes.bfloat16
    cosq = (cos_t.T * s).astype(bf)
    sinq = (np.concatenate([-sin_t.T[:64], sin_t.T[64:]], axis=0) * s).astype(bf)
    cosk = cos_t.T.astype(bf)
    sink = np.concatenate([-sin_t.T[:64], sin_t.T[64:]], axis=0).astype(bf)
    ident = np.eye(P).astype(bf)
    onesc = np.ones((P, 1), dtype=bf)
    onesrf = np.ones((1, P), dtype=np.float32)
    _CACHE["c"] = (cosq, sinq, cosk, sink, ident, onesc, onesrf)
    return _CACHE["c"]


def _program():
    if "nc" in _CACHE:
        return _CACHE["nc"]
    ms1 = MS()
    build(ms1)
    ms2 = MS(prev=ms1.out, semof_prev=ms1.semof)
    nc = build(ms2)
    assert ms1.cnt == ms2.cnt, "pass mismatch"
    _CACHE["nc"] = nc
    return nc


def kernel(L0_attn_x, L0_attn_Wq, L0_attn_Wk, L0_attn_Wv, L0_attn_Wout,
           L0_ffn_gate_W_packed, L0_ffn_gate_scales, L0_ffn_up_W_packed,
           L0_ffn_up_scales, L0_ffn_down_W_packed, L0_ffn_down_scales):
    x = np.ascontiguousarray(np.asarray(L0_attn_x, dtype=np.float32))
    wqh = np.ascontiguousarray(np.asarray(L0_attn_Wq, dtype=np.float32))
    wkh = np.ascontiguousarray(np.asarray(L0_attn_Wk, dtype=np.float32))
    wvh = np.ascontiguousarray(np.asarray(L0_attn_Wv, dtype=np.float32))
    woh = np.ascontiguousarray(np.asarray(L0_attn_Wout, dtype=np.float32))
    gpk = np.ascontiguousarray(np.asarray(L0_ffn_gate_W_packed, dtype=np.int32))
    gsc = np.ascontiguousarray(np.asarray(L0_ffn_gate_scales, dtype=np.float32))
    upk = np.ascontiguousarray(np.asarray(L0_ffn_up_W_packed, dtype=np.int32))
    usc = np.ascontiguousarray(np.asarray(L0_ffn_up_scales, dtype=np.float32))
    dpk = np.ascontiguousarray(np.asarray(L0_ffn_down_W_packed, dtype=np.int32))
    dsc = np.ascontiguousarray(np.asarray(L0_ffn_down_scales, dtype=np.float32))
    cosq, sinq, cosk, sink, ident, onesc, onesrf = _consts()
    epsv = np.full((P, 1), 1e-6, dtype=np.float32)

    nc = _program()
    in_maps = []
    for c in range(8):
        b, th = c // 2, c % 2
        xown = np.ascontiguousarray(x[b, th * TO:(th + 1) * TO])
        xoth = np.ascontiguousarray(x[b, (1 - th) * TO:(2 - th) * TO])
        tq = np.ascontiguousarray(np.concatenate(
            [cosq[:, th * TO:(th + 1) * TO], sinq[:, th * TO:(th + 1) * TO]], axis=1))
        ckp = np.concatenate([cosk[:, th * TO:(th + 1) * TO],
                              cosk[:, (1 - th) * TO:(2 - th) * TO]], axis=1)
        skp = np.concatenate([sink[:, th * TO:(th + 1) * TO],
                              sink[:, (1 - th) * TO:(2 - th) * TO]], axis=1)
        tk = np.ascontiguousarray(np.concatenate([ckp, skp], axis=1))
        in_maps.append(dict(
            xo=xown, xz=xoth, wq=wqh, wk=wkh, wv=wvh, wo=woh,
            gpk=np.ascontiguousarray(gpk[c * NSL:(c + 1) * NSL]),
            gsc=np.ascontiguousarray(gsc[c * NSL:(c + 1) * NSL]),
            upk=np.ascontiguousarray(upk[c * NSL:(c + 1) * NSL]),
            usc=np.ascontiguousarray(usc[c * NSL:(c + 1) * NSL]),
            dpk=np.ascontiguousarray(dpk[:, c * 1024:(c + 1) * 1024]),
            dsc=np.ascontiguousarray(dsc[:, c * 64:(c + 1) * 64]),
            identd=ident, onescd=onesc, onesrfd=onesrf, epsd=epsv,
            tblq=tq, tblk=tk))
    res = run_bass_kernel_spmd(nc, in_maps, core_ids=list(range(8)))
    outv = np.empty((4, TA, C), dtype=np.float32)
    for c in range(8):
        b, th = c // 2, c % 2
        outv[b, th * TO:(th + 1) * TO] = res.results[c]["out"]
    return outv


# version 21
# speedup vs baseline: 1.0215x; 1.0200x over previous
import contextlib
import os
import numpy as np
import ml_dtypes
import concourse.bass as bass
import concourse.mybir as mybir
from concourse.bass_utils import run_bass_kernel_spmd

F32 = mybir.dt.float32
BF16 = mybir.dt.bfloat16
I32 = mybir.dt.int32
AF = mybir.ActivationFunctionType
ALU = mybir.AluOpType
AX = mybir.AxisListType.X

P = 128
TA = 2048
TO = 1024
C = 2048
KC = 512
MLP = 16384
NSL = 2048
NSUB = 96
NPUMP = 101
KLV = int(os.environ.get("KLV", "9"))
KCUT = int(os.environ.get("KCUT", "9"))
KDBG = os.environ.get("KDBG", "0") == "1"


class Sem:
    def __init__(self, h, name):
        self.h, self.name = h, name


class MS:
    def __init__(self, prev=None, semof_prev=None):
        self.prev = prev if prev is not None else {}
        self.semof_prev = semof_prev if semof_prev is not None else {}
        self.out = {}
        self.cnt = {}
        self.semof = {}

    def inc(self, inst, sem, k=1, name=None):
        inst.then_inc(sem.h, k)
        self.cnt[sem.name] = self.cnt.get(sem.name, 0) + k
        if name is not None:
            assert name not in self.out, name
            self.out[name] = self.cnt[sem.name]
            self.semof[name] = sem.name
        return self.cnt[sem.name]

    def mark(self, sem, name):
        assert name not in self.out, name
        self.out[name] = self.cnt.get(sem.name, 0)
        self.semof[name] = sem.name

    def wait(self, eng, sem, name):
        so = self.semof_prev.get(name)
        assert so is None or so == sem.name, (name, so, sem.name)
        v = self.prev.get(name, 0)
        if v > 0:
            eng.wait_ge(sem.h, v)


def build(ms):
    nc = bass.Bass("TRN2", target_bir_lowering=False, debug=False)
    xo = nc.dram_tensor("xo", [TO, C], F32, kind="ExternalInput").ap()
    xz = nc.dram_tensor("xz", [TO, C], F32, kind="ExternalInput").ap()
    wq = nc.dram_tensor("wq", [C, C], F32, kind="ExternalInput").ap()
    wk = nc.dram_tensor("wk", [C, KC], F32, kind="ExternalInput").ap()
    wv = nc.dram_tensor("wv", [C, KC], F32, kind="ExternalInput").ap()
    wo = nc.dram_tensor("wo", [C, C], F32, kind="ExternalInput").ap()
    pks = {"g": nc.dram_tensor("gpk", [NSL, 1024], I32, kind="ExternalInput").ap(),
           "u": nc.dram_tensor("upk", [NSL, 1024], I32, kind="ExternalInput").ap(),
           "d": nc.dram_tensor("dpk", [C, 1024], I32, kind="ExternalInput").ap()}
    scs = {"g": nc.dram_tensor("gsc", [NSL, 64], F32, kind="ExternalInput").ap(),
           "u": nc.dram_tensor("usc", [NSL, 64], F32, kind="ExternalInput").ap(),
           "d": nc.dram_tensor("dsc", [C, 64], F32, kind="ExternalInput").ap()}
    identd = nc.dram_tensor("identd", [P, P], BF16, kind="ExternalInput").ap()
    onescd = nc.dram_tensor("onescd", [P, 1], BF16, kind="ExternalInput").ap()
    onesrfd = nc.dram_tensor("onesrfd", [1, P], F32, kind="ExternalInput").ap()
    tblq = nc.dram_tensor("tblq", [P, 2 * TO], BF16, kind="ExternalInput").ap()
    epsd = nc.dram_tensor("epsd", [P, 1], F32, kind="ExternalInput").ap()
    tblk = nc.dram_tensor("tblk", [P, 2 * TA], BF16, kind="ExternalInput").ap()
    out = nc.dram_tensor("out", [TO, C], F32, kind="ExternalOutput").ap()
    if KDBG:
        d_q = nc.dram_tensor("d_q", [16 * P, TO], BF16, kind="ExternalOutput").ap()
        d_k = nc.dram_tensor("d_k", [P, 8192], BF16, kind="ExternalOutput").ap()
        d_v = nc.dram_tensor("d_v", [P, 8192], BF16, kind="ExternalOutput").ap()
        d_qp = nc.dram_tensor("d_qp", [16 * P, TO], BF16, kind="ExternalOutput").ap()
        d_enc = nc.dram_tensor("d_enc", [P, 16384], BF16, kind="ExternalOutput").ap()
        d_h = nc.dram_tensor("d_h", [P, 16384], BF16, kind="ExternalOutput").ap()
        d_eb = nc.dram_tensor("d_eb", [P, 8192], BF16, kind="ExternalOutput").ap()
        d_wg = nc.dram_tensor("d_wg", [128, P * 2048], BF16, kind="ExternalOutput").ap()
        d_wd = nc.dram_tensor("d_wd", [MLP, C], BF16, kind="ExternalOutput").ap()

    q_sp = nc.dram_tensor("q_sp", [16 * P, TO], BF16, kind="Internal")
    wo_bf = nc.dram_tensor("wo_bf", [P, 32768], BF16, kind="Internal")
    agi = {"g": nc.dram_tensor("agi_g", [16, P * 2048], BF16, kind="Internal"),
           "u": nc.dram_tensor("agi_u", [16, P * 2048], BF16, kind="Internal")}
    ago = {"g": nc.dram_tensor("ago_g", [128, P * 2048], BF16, kind="Internal"),
           "u": nc.dram_tensor("ago_u", [128, P * 2048], BF16, kind="Internal")}
    agi_dh = [nc.dram_tensor(f"agi_d{h}", [NSL, 1024], BF16, kind="Internal")
              for h in range(2)]
    ago_dh = [nc.dram_tensor(f"ago_d{h}", [MLP, 1024], BF16, kind="Internal")
              for h in range(2)]

    st = contextlib.ExitStack()
    e = st.enter_context
    with st:
        s_id = e(nc.sbuf_tensor("s_id", [P, P], BF16))
        s_oc = e(nc.sbuf_tensor("s_oc", [P, 1], BF16))
        s_orf = e(nc.sbuf_tensor("s_orf", [1, P], F32))
        EPSB = e(nc.sbuf_tensor("EPSB", [P, 1], F32))
        A2 = e(nc.sbuf_tensor("A2", [P, 16384], BF16))
        A3 = e(nc.sbuf_tensor("A3", [P, 16384], BF16))
        A4 = e(nc.sbuf_tensor("A4", [P, 8192], BF16))
        A5 = e(nc.sbuf_tensor("A5", [P, 8192], BF16))
        A6 = e(nc.sbuf_tensor("A6", [P, 8192], BF16))
        A7 = e(nc.sbuf_tensor("A7", [P, 8192], BF16))
        WQB = e(nc.sbuf_tensor("WQB", [P, 4096], BF16))
        QEV = e(nc.sbuf_tensor("QEV", [P, 2048], BF16))
        QBC = e(nc.sbuf_tensor("QBC", [P, 1024], BF16))
        XT = e(nc.sbuf_tensor("XT", [P, 4096], F32))
        SQB = e(nc.sbuf_tensor("SQB", [P, 2048], BF16))
        H2 = e(nc.sbuf_tensor("H2", [P, 2048], BF16))
        RT = e(nc.sbuf_tensor("RT", [P, 2048], BF16))
        STA = e(nc.sbuf_tensor("STA", [P, 16], F32))
        ROWS = e(nc.sbuf_tensor("ROWS", [1, 3072], F32))
        RBS = e(nc.sbuf_tensor("RBS", [P, 1024], F32))
        DPK = e(nc.sbuf_tensor("DPK", [P, 1024], I32))
        DTS = e(nc.sbuf_tensor("DTS", [P, 2048], BF16))
        DS4 = e(nc.sbuf_tensor("DS4", [P, 128], F32))
        DSX = e(nc.sbuf_tensor("DSX", [P, 1024], BF16))
        DW = e(nc.sbuf_tensor("DW", [P, 2048], BF16))
        DWT = e(nc.sbuf_tensor("DWT", [P, 2048], BF16))

        def mksem(nm):
            return Sem(e(nc.semaphore(nm)), nm)

        def mksem2(nm):
            return [mksem(nm + "0"), mksem(nm + "1")]

        sV = mksem("sV")
        sA = mksem("sA")
        sT = mksem("sT")
        sAG = mksem("sAG")
        sCONST = mksem("sCONST")
        sWKV = mksem("sWKV")
        sWQ = mksem2("sWQ")
        sXL = mksem2("sXL")
        sQS = mksem2("sQS")
        sKSW = mksem("sKSW")
        sQH = mksem2("sQH")
        sPK = mksem2("sPK")
        sSC = mksem2("sSC")
        sDW = mksem2("sDW")
        sDWA = mksem2("sDWA")
        sXC = mksem("sXC")
        sWO = mksem2("sWO")
        sO1 = mksem2("sO1")
        sXR = mksem2("sXR")
        sWG = mksem2("sWG")
        sWD = mksem2("sWD")
        sAC = mksem2("sAC")

        kfm = A5
        vtm = A7
        EB = A6
        A4I = A4[:].bitcast(I32)
        DCC = A4I[:, 0:1024]
        DB1 = A4I[:, 1024:2048]
        DB2 = A4I[:, 2048:3072]
        DB3 = A4I[:, 3072:4096]

        def lnx(j):
            return A3[:, (j % 2) * 8192:(j % 2) * 8192 + 8192]

        cosq = A2[:, 0:1024]
        sinq = A2[:, 1024:2048]

        def qevsl(hd):
            k = hd % 3
            return QEV[:, k * 1024:k * 1024 + 1024] if k < 2 \
                else A2[:, 2048:3072]

        def rtsl(hd):
            k = hd % 3
            return RT[:, k * 1024:k * 1024 + 1024] if k < 2 \
                else A2[:, 3072:4096]
        cosk = A2[:, 4096:6144]
        sink = A2[:, 6144:8192]

        def dsub(i):
            return "gud"[i // 32], (i % 32) // 2, (i % 32) % 2

        def dec_load(g, i):
            if i < 0 or i >= NSUB:
                return
            m, rt, hf = dsub(i)
            ms.wait(g, sV, f"dv_{i-2}")
            sl = (i % 2) * 512
            inst = g.dma_start(out=DPK[:, sl:sl + 512],
                               in_=pks[m][rt * P:(rt + 1) * P, hf * 512:(hf + 1) * 512])
            ms.inc(inst, sPK[i % 2], 16, name=f"dl_{i}")
            if hf == 0:
                ssl = (rt % 2) * 64
                ms.wait(g, sV, f"dv_{i-3}")
                inst = g.dma_start(out=DS4[:, ssl:ssl + 64],
                                   in_=scs[m][rt * P:(rt + 1) * P, :])
                ms.inc(inst, sSC[rt % 2], 16, name=f"scl_{i}")

        def dec_v(v, i):
            if i < 0 or i >= NSUB:
                return
            m, rt, hf = dsub(i)
            ms.wait(v, sPK[i % 2], f"dl_{i}")
            ms.wait(v, sSC[rt % 2], f"scl_{i - hf}")
            pk = DPK[:, (i % 2) * 512:(i % 2) * 512 + 512]
            Cc = DCC
            v.tensor_single_scalar(DCC[:, 0:1024:2], pk, 15, ALU.bitwise_and)
            v.tensor_single_scalar(DCC[:, 1:1024:2], pk, 4, ALU.logical_shift_right)
            v.tensor_single_scalar(DB1, Cc, 7, ALU.bitwise_and)
            v.tensor_single_scalar(DB2, DB1, 1, ALU.logical_shift_right)
            v.tensor_scalar(DB3, Cc, 1, 2, op0=ALU.bitwise_and, op1=ALU.bitwise_or)
            v.tensor_tensor(DB3, DB3, DB2, ALU.logical_shift_left)
            v.tensor_scalar(DB1, DB1, 2, -2, op0=ALU.min, op1=ALU.add)
            v.tensor_tensor(DB3, DB3, DB1, ALU.add)
            v.tensor_single_scalar(DB2, Cc, 3, ALU.logical_shift_right)
            v.tensor_scalar(DB2, DB2, -2, 1, op0=ALU.mult, op1=ALU.add)
            tss = DTS[:, (i % 2) * 1024:(i % 2) * 1024 + 1024]
            v.tensor_tensor(tss, DB3, DB2, ALU.mult)
            ssl = (rt % 2) * 64
            sc4 = DS4[:, ssl + hf * 32: ssl + hf * 32 + 32]
            v.tensor_single_scalar(DSX[:].rearrange("p (a b) -> p a b", b=32),
                                   sc4.unsqueeze(2).to_broadcast((P, 32, 32)),
                                   0.25, ALU.mult)
            ms.wait(v, sT, f"dt_{i-2}")
            dws = DW[:, (i % 2) * 1024:(i % 2) * 1024 + 1024]
            inst = v.tensor_tensor(dws, tss, DSX[:], ALU.mult)
            ms.inc(inst, sV, 1, name=f"dv_{i}")

        def dec_t(t, pdec, i):
            if i < 0 or i >= NSUB:
                return
            ms.wait(t, sV, f"dv_{i}")
            ms.wait(t, sV, f"de_{i-2}")
            dws = DW[:, (i % 2) * 1024:(i % 2) * 1024 + 1024]
            pb = pdec[i % 2]
            last = None
            for j in range(8):
                last = t.transpose(pb[:, j * P:(j + 1) * P], dws[:, j * P:(j + 1) * P],
                                   s_id[:])
            ms.inc(last, sT, 1, name=f"dt_{i}")

        def dec_e(v, pdec, i):
            if i < 0 or i >= NSUB:
                return
            ms.wait(v, sT, f"dt_{i}")
            if i >= 2:
                ms.wait(v, sDW[i % 2], f"dw_{i-2}")
            inst = v.tensor_copy(DWT[:, (i % 2) * 1024:(i % 2) * 1024 + 1024],
                                 pdec[i % 2][:])
            ms.inc(inst, sV, 1, name=f"de_{i}")

        def dec_w(g, i):
            if i < 0 or i >= NSUB:
                return
            m, rt, hf = dsub(i)
            ms.wait(g, sV, f"de_{i}")
            src = DWT[:, (i % 2) * 1024:(i % 2) * 1024 + 1024]
            if m in ("g", "u"):
                dst = agi[m].ap().rearrange("a (p f) -> a p f", p=P)[
                    rt, :, hf * 1024:hf * 1024 + 1024]
                ms.inc(g.dma_start(out=dst, in_=src), sDW[i % 2], 16,
                       name=f"dw_{i}")
            else:
                adh = agi_dh[rt // 8].ap()
                for j in range(8):
                    piece = src[:, j * P:(j + 1) * P]
                    dst = adh[hf * 1024 + j * P:hf * 1024 + (j + 1) * P,
                              (rt % 8) * P:(rt % 8 + 1) * P]
                    nm = f"dw_{i}" if j == 7 else None
                    ms.inc(g.dma_start(out=dst, in_=piece), sDW[i % 2], 16,
                           name=nm)

        def ag_trig(g, m2):
            iN = {"g": 31, "u": 63, "d1": 79, "d2": 95}[m2]
            ms.wait(g, sDW[iN % 2], f"dw_{iN}")
            if m2 in ("d1", "d2"):
                src_, dst_ = agi_dh[0 if m2 == "d1" else 1], \
                    ago_dh[0 if m2 == "d1" else 1]
            else:
                src_, dst_ = agi[m2], ago[m2]
            inst = g.collective_compute(
                "AllGather", ALU.bypass,
                replica_groups=[[0, 1, 2, 3, 4, 5, 6, 7]],
                ins=[src_.ap().opt()], outs=[dst_.ap().opt()])
            ms.inc(inst, sAG, 1, name=f"ag_{m2}")

        def pump_g(g, p):
            dec_load(g, p)
            dec_w(g, p - 5)
            if p - 5 == 31:
                ag_trig(g, "g")

        def pump_v(v, pdec, p):
            dec_v(v, p - 2)
            dec_e(v, pdec, p - 4)

        def pump_t(t, pdec, p):
            dec_t(t, pdec, p - 3)

        with nc.Block() as blk:
            @blk.sync
            def _(s):
                for src, dst in ((identd, s_id[:]), (onescd, s_oc[:]),
                                 (onesrfd, s_orf[:]), (epsd, EPSB[:])):
                    ms.inc(s.dma_start(out=dst, in_=src), sCONST, 16)
                ms.inc(s.dma_start(out=A2[:, 0:2048], in_=tblq), sCONST, 16)
                ms.inc(s.dma_start(out=A2[:, 4096:8192], in_=tblk), sCONST, 16)
                ms.mark(sCONST, "const")

            @blk.gpsimd
            def _(g):
                src = wk.rearrange("(ch p) n -> p ch n", p=P)
                ms.inc(g.dma_start(out=A4[:].rearrange("p (a b) -> p a b", b=KC),
                                   in_=src), sWKV, 16)
                src = wv.rearrange("(ch p) n -> p ch n", p=P)
                ms.inc(g.dma_start(out=A6[:].rearrange("p (a b) -> p a b", b=KC),
                                   in_=src), sWKV, 16)
                ms.mark(sWKV, "wkv")

        ps1 = contextlib.ExitStack()
        pT = [ps1.enter_context(nc.psum_tensor(f"pT{i}", [P, 1024], BF16))
              for i in range(2)]
        pQ = [ps1.enter_context(nc.psum_tensor(f"pQ{i}", [P, 512], F32))
              for i in range(2)]
        pKV = [ps1.enter_context(nc.psum_tensor(f"pKV{i}", [P, 512], F32))
               for i in range(2)]
        pR = [ps1.enter_context(nc.psum_tensor(f"pR{i}", [1, 512], F32))
              for i in range(2)]

        def vsync(v, inst):
            c = ms.inc(inst, sV, 1)
            v.wait_ge(sV.h, c)

        def asyncf(a, inst):
            c = ms.inc(inst, sA, 1)
            a.wait_ge(sA.h, c)

        def ln_stats_v(v, xs, tag):
            vsync(v, v.reduce_sum(STA[:, 0:1], xs, axis=AX))
            vsync(v, v.tensor_single_scalar(STA[:, 1:2], STA[:, 0:1], 1.0 / C,
                                            ALU.mult))
            vsync(v, v.tensor_tensor(STA[:, 8:9], STA[:, 1:2], STA[:, 1:2], ALU.mult))
            inst = v.tensor_scalar(STA[:, 2:3], STA[:, 8:9], -1.0, 1e-5,
                                   op0=ALU.mult, op1=ALU.add)
            ms.inc(inst, sV, 1, name=f"lm_{tag}")

        def ln_stats_a(a, xs, tag):
            ms.wait(a, sV, f"lm_{tag}")
            asyncf(a, a.activation(SQB[:], xs, AF.Square, accum_out=STA[:, 3:4]))
            asyncf(a, a.activation(STA[:, 7:8], STA[:, 3:4], AF.Ln, bias=STA[:, 2:3],
                                   scale=1.0 / C))
            inst = a.activation(STA[:, 5:6], STA[:, 7:8], AF.Exp, scale=-0.5)
            ms.inc(inst, sA, 1, name=f"lr_{tag}")

        def lnx_evac(v, i):
            j = i // 4
            tt = i % 4
            ms.wait(v, sT, f"lt_{i}")
            dst = lnx(j).rearrange("p (a b) -> p a b", b=512)
            last = None
            for gi in range(2):
                src = pT[gi][:].rearrange("p (a b) -> p a b", b=P)
                d3 = dst[:, gi * 8:gi * 8 + 8, tt * P:tt * P + P]
                last = v.tensor_copy(d3, src)
            ms.inc(last, sV, 1, name=f"le_{i}")

        with nc.Block() as blk:
            @blk.gpsimd
            def _(g):
                for qc in range(16 if KCUT >= 3 else 0):
                    if qc >= 2:
                        ms.wait(g, sT, f"qmm_{qc-2}_1")
                    srcw = wq[:, qc * P:(qc + 1) * P].rearrange("(ch p) n -> p ch n",
                                                                p=P)
                    dst = WQB[:, (qc % 2) * 2048:(qc % 2) * 2048 + 2048]
                    inst = g.dma_start(out=dst.rearrange("p (a b) -> p a b", b=P),
                                       in_=srcw)
                    ms.inc(inst, sWQ[qc % 2], 16, name=f"wq_{qc}")

            @blk.sync
            def _(s):
                def xld(i):
                    ms.wait(s, sV, f"lh_x{i-2}")
                    src = (xo if i < 8 else xz)[(i % 8) * P:(i % 8) * P + P, :]
                    inst = s.dma_start(out=XT[:, (i % 2) * C:(i % 2) * C + C], in_=src)
                    ms.inc(inst, sXL[i % 2], 16, name=f"xld_{i}")

                for i in range(10):
                    xld(i)
                for qc in range(16 if KCUT >= 3 else 0):
                    ms.wait(s, sV, f"qev_{qc}_1")
                    src = QEV[:, (qc % 2) * 1024:(qc % 2) * 1024 + 1024]
                    inst = s.dma_start(out=q_sp.ap()[qc * P:(qc + 1) * P, :], in_=src)
                    ms.inc(inst, sQS[qc % 2], 16, name=f"qsp_{qc}")
                for i in range(10, 16):
                    xld(i)
                for kc in range(4 if KCUT >= 4 else 0):
                    ms.wait(s, sV, f"krn_{kc}")
                    ms.inc(s.dma_start(out=SQB[0:64, :], in_=RT[64:128, 0:2048]), sKSW, 16)
                    inst = s.dma_start(out=SQB[64:128, :], in_=RT[0:64, 0:2048])
                    ms.inc(inst, sKSW, 16, name=f"ksw_{kc}")

            @blk.vector
            def _(v):
                ms.wait(v, sCONST, "const")

                def ln_tile(i):
                    xs = XT[:, (i % 2) * C:(i % 2) * C + C]
                    ms.wait(v, sXL[i % 2], f"xld_{i}")
                    ln_stats_v(v, xs, f"x{i}")
                    ms.wait(v, sA, f"lr_x{i}")
                    ms.wait(v, sT, f"lt_{i-1}")
                    inst = v.tensor_scalar(H2[:], xs, STA[:, 1:2], STA[:, 5:6],
                                           op0=ALU.subtract, op1=ALU.mult)
                    ms.inc(inst, sV, 1, name=f"lh_x{i}")
                    if i >= 1 and f"le_{i-1}" not in ms.out:
                        lnx_evac(v, i - 1)

                def kv_evac(j):
                    for kc in range(4):
                        ms.wait(v, sT, f"kmm_{j}_{kc}")
                        inst = v.tensor_copy(
                            kfm[:, kc * 2048 + j * 512:kc * 2048 + j * 512 + 512],
                            pKV[kc % 2][:])
                        ms.inc(inst, sV, 1, name=f"kev_{j}_{kc}")
                    for t4 in range(4):
                        ms.wait(v, sT, f"vmm_{j}_{t4}")
                        inst = v.tensor_copy(
                            vtm[:, (j * 4 + t4) * 512:(j * 4 + t4) * 512 + 512],
                            pKV[t4 % 2][:])
                        ms.inc(inst, sV, 1, name=f"vev_{j}_{t4}")

                for i in range(8):
                    ln_tile(i)
                lnx_evac(v, 7)
                if KCUT >= 2:
                    kv_evac(0)
                    kv_evac(1)
                for qc in range(16 if KCUT >= 3 else 0):
                    for jb in range(2):
                        ms.wait(v, sT, f"qmm_{qc}_{jb}")
                        if qc >= 2 and jb == 0:
                            ms.wait(v, sQS[qc % 2], f"qsp_{qc-2}")
                            ms.wait(v, sA, f"qsq_{qc-2}")
                        dst = QEV[:, (qc % 2) * 1024 + jb * 512:
                                  (qc % 2) * 1024 + jb * 512 + 512]
                        inst = v.tensor_copy(dst, pQ[jb][:])
                        nm = f"qev_{qc}_{jb}" if jb == 1 else None
                        ms.inc(inst, sV, 1, name=nm)
                for i in range(8, 12):
                    ln_tile(i)
                lnx_evac(v, 11)
                if KCUT >= 2:
                    kv_evac(2)
                for i in range(12, 16):
                    ln_tile(i)
                lnx_evac(v, 15)
                if KCUT >= 2:
                    kv_evac(3)
                if KCUT < 4:
                    return
                for half in range(2):
                    ms.wait(v, sT, f"qbc_{half}")
                    inst = v.tensor_copy(QBC[:, half * 512:half * 512 + 512], pQ[0][:])
                    ms.inc(inst, sV, 1, name=f"qbcc_{half}")
                for sh in range(4):
                    ms.wait(v, sT, f"kbc_{sh}")
                    inst = v.tensor_copy(XT[:, sh * 512:sh * 512 + 512], pQ[1][:])
                    ms.inc(inst, sV, 1, name=f"kbcc_{sh}")
                for kc in range(4):
                    kch = kfm[:, kc * 2048:kc * 2048 + 2048]
                    inst = v.tensor_tensor(RT[:, 0:2048], kch, XT[:, 0:2048], ALU.mult)
                    ms.inc(inst, sV, 1, name=f"krn_{kc}")
                    v.tensor_tensor(H2[:], RT[:, 0:2048], cosk, ALU.mult)
                    ms.wait(v, sKSW, f"ksw_{kc}")
                    v.tensor_tensor(RT[:, 0:2048], SQB[:], sink, ALU.mult)
                    inst = v.tensor_tensor(kch, H2[:], RT[:, 0:2048], ALU.add)
                    ms.inc(inst, sV, 1, name=f"krope_{kc}")
                ms.mark(sV, "krope")

            @blk.scalar
            def _(a):
                for i in range(8):
                    xs = XT[:, (i % 2) * C:(i % 2) * C + C]
                    ln_stats_a(a, xs, f"x{i}")
                for qc in range(16 if KCUT >= 3 else 0):
                    ms.wait(a, sV, f"qev_{qc}_1")
                    if qc >= 1:
                        ms.wait(a, sT, f"qsum_{qc-1}")
                    inst = a.activation(SQB[:, 0:1024],
                                        QEV[:, (qc % 2) * 1024:(qc % 2) * 1024 + 1024],
                                        AF.Square)
                    ms.inc(inst, sA, 1, name=f"qsq_{qc}")
                if KCUT >= 3:
                    ms.wait(a, sT, "qsumF")
                for i in range(8, 16):
                    xs = XT[:, (i % 2) * C:(i % 2) * C + C]
                    ln_stats_a(a, xs, f"x{i}")
                if KCUT < 4:
                    return
                for half in range(2):
                    ms.wait(a, sT, "qsumF")
                    if half == 1:
                        ms.wait(a, sA, "qrow_0")
                    asyncf(a, a.activation(ROWS[0:1, 2048:2560],
                                           pR[half][0:1, :], AF.Ln,
                                           bias=EPSB[0:1, 0:1], scale=1.0 / C))
                    inst = a.activation(ROWS[0:1, half * 512:half * 512 + 512],
                                        ROWS[0:1, 2048:2560],
                                        AF.Exp, scale=-0.5)
                    ms.inc(inst, sA, 1, name=f"qrow_{half}")
                for ph in range(2):
                    for kc in range(4):
                        ms.wait(a, sV, f"kev_3_{kc}")
                        if not (ph == 0 and kc == 0):
                            pv = f"ksum_{ph}_{kc-1}" if kc >= 1 else "ksumF_0"
                            ms.wait(a, sT, pv)
                        inst = a.activation(SQB[:, 0:2048],
                                            kfm[:, kc * 2048:kc * 2048 + 2048],
                                            AF.Square)
                        ms.inc(inst, sA, 1, name=f"ksq_{ph}_{kc}")
                    for sh2 in range(2):
                        sh = ph * 2 + sh2
                        ms.wait(a, sT, f"ksumF_{ph}")
                        r = ROWS[0:1, 1024 + sh * 512:1024 + sh * 512 + 512]
                        a.activation(r, pR[sh2][0:1, :], AF.Ln,
                                     bias=EPSB[0:1, 0:1], scale=1.0 / KC)
                        inst = a.activation(r, r, AF.Exp, scale=-0.5)
                        ms.inc(inst, sA, 1, name=f"krow_{sh}")

            @blk.tensor
            def _(t):
                ms.wait(t, sCONST, "const")
                ms.wait(t, sWKV, "wkv")

                def tile_trans(i):
                    ms.wait(t, sV, f"lh_x{i}")
                    ms.wait(t, sV, f"le_{i-1}")
                    last = None
                    for gi in range(2):
                        for j in range(8):
                            cch = gi * 8 + j
                            last = t.transpose(pT[gi][:, j * P:(j + 1) * P],
                                               H2[:, cch * P:(cch + 1) * P], s_id[:])
                    ms.inc(last, sT, 1, name=f"lt_{i}")

                def kv_block(j):
                    for kc in range(4):
                        if kc >= 2:
                            ms.wait(t, sV, f"kev_{j}_{kc-2}")
                        if j >= 1:
                            ms.wait(t, sV, f"kev_{j-1}_{kc}")
                            ms.wait(t, sV, f"vev_{j-1}_{2 + (kc % 2)}")
                        lastm = None
                        for cch in range(16):
                            lastm = t.matmul(
                                pKV[kc % 2][:],
                                A4[:, cch * 512 + kc * 128:cch * 512 + kc * 128 + 128],
                                lnx(j)[:, cch * 512:cch * 512 + 512],
                                start=(cch == 0), stop=(cch == 15))
                        ms.inc(lastm, sT, 1, name=f"kmm_{j}_{kc}")
                    for t4 in range(4):
                        if t4 >= 2:
                            ms.wait(t, sV, f"vev_{j}_{t4-2}")
                        if j >= 1:
                            ms.wait(t, sV, f"vev_{j-1}_{2 + (t4 % 2)}")
                        ms.wait(t, sV, f"kev_{j}_{2 + (t4 % 2)}")
                        lastm = None
                        for cch in range(16):
                            lastm = t.matmul(
                                pKV[t4 % 2][:],
                                lnx(j)[:, cch * 512 + t4 * 128:cch * 512 + t4 * 128 + 128],
                                A6[:, cch * 512:cch * 512 + 512],
                                start=(cch == 0), stop=(cch == 15))
                        ms.inc(lastm, sT, 1, name=f"vmm_{j}_{t4}")

                for i in range(8):
                    tile_trans(i)
                for j in (0, 1):
                    ms.wait(t, sV, f"le_{j * 4 + 3}")
                    kv_block(j)
                def qsum_mm(qs):
                    ms.wait(t, sA, f"qsq_{qs}")
                    lastm = None
                    for half in range(2):
                        lastm = t.matmul(pR[half][0:1, :], s_oc[:, 0:1],
                                         SQB[:, half * 512:half * 512 + 512],
                                         start=(qs == 0), stop=(qs == 15))
                    nm = "qsumF" if qs == 15 else f"qsum_{qs}"
                    ms.inc(lastm, sT, 1, name=nm)

                for qc in range(16 if KCUT >= 3 else 0):
                    ms.wait(t, sWQ[qc % 2], f"wq_{qc}")
                    if qc >= 1:
                        ms.wait(t, sV, f"qev_{qc-1}_1")
                    lastm = None
                    for cch in range(16):
                        for jb in range(2):
                            lastm = t.matmul(
                                pQ[jb][:],
                                WQB[:, (qc % 2) * 2048 + cch * 128:
                                    (qc % 2) * 2048 + cch * 128 + 128],
                                lnx(jb)[:, cch * 512:cch * 512 + 512],
                                start=(cch == 0), stop=(cch == 15))
                    ms.inc(lastm, sT, 1, name=f"qmm_{qc}_1")
                    ms.mark(sT, f"qmm_{qc}_0")
                    if qc >= 1:
                        qsum_mm(qc - 1)
                if KCUT >= 3:
                    qsum_mm(15)
                for i in range(8, 12):
                    tile_trans(i)
                if KCUT >= 2:
                    ms.wait(t, sV, "le_11")
                    kv_block(2)
                for i in range(12, 16):
                    tile_trans(i)
                if KCUT >= 2:
                    ms.wait(t, sV, "le_15")
                    kv_block(3)
                if KCUT < 4:
                    return
                for half in range(2):
                    ms.wait(t, sA, f"qrow_{half}")
                    if half >= 1:
                        ms.wait(t, sV, "qbcc_0")
                    rows = ROWS[0:1, half * 512:half * 512 + 512]
                    lastm = t.matmul(pQ[0][:], s_orf[0:1, :], rows,
                                     start=True, stop=True)
                    ms.inc(lastm, sT, 1, name=f"qbc_{half}")
                for ph in range(2):
                    for kc in range(4):
                        ms.wait(t, sA, f"ksq_{ph}_{kc}")
                        lastm = None
                        for sh2 in range(2):
                            sh = ph * 2 + sh2
                            lastm = t.matmul(pR[sh2][0:1, :], s_oc[:, 0:1],
                                             SQB[:, sh * 512:sh * 512 + 512],
                                             start=(kc == 0), stop=(kc == 3))
                        nm = f"ksumF_{ph}" if kc == 3 else f"ksum_{ph}_{kc}"
                        ms.inc(lastm, sT, 1, name=nm)
                    for sh2 in range(2):
                        sh = ph * 2 + sh2
                        ms.wait(t, sA, f"krow_{sh}")
                        if sh >= 1:
                            ms.wait(t, sV, f"kbcc_{sh-1}")
                        rows = ROWS[0:1, 1024 + sh * 512:1024 + sh * 512 + 512]
                        lastm = t.matmul(pQ[1][:], s_orf[0:1, :], rows,
                                         start=True, stop=True)
                        ms.inc(lastm, sT, 1, name=f"kbc_{sh}")

        ps1.close()

        if KLV < 2:
            return nc
        psD = contextlib.ExitStack()
        pDEC = [psD.enter_context(nc.psum_tensor(f"pDEC{i}", [P, 1024], BF16))
                for i in range(2)]

        ps2 = contextlib.ExitStack()
        pL = [ps2.enter_context(nc.psum_tensor(f"pL{i}", [P, 512], F32))
              for i in range(2)]
        pS = ps2.enter_context(nc.psum_tensor("pS", [1, 512], F32))
        pE = [ps2.enter_context(nc.psum_tensor(f"pE{i}", [P, 512], F32))
              for i in range(2)]
        pB = ps2.enter_context(nc.psum_tensor("pB", [P, 512], F32))

        with nc.Block() as blk:
            @blk.sync
            def _(s):
                for hd in range(16):
                    if hd >= 3:
                        ms.wait(s, sT, f"lg_{(hd - 3) * 2 + 1}_15")
                        if KDBG:
                            ms.wait(s, sCONST, f"dbq_{hd-3}")
                    else:
                        ms.wait(s, sQS[1], "qsp_15")
                        ms.wait(s, sA, "qsq_15")
                        ms.wait(s, sV, "krope")
                    inst = s.dma_start(out=qevsl(hd),
                                       in_=q_sp.ap()[hd * P:(hd + 1) * P, :])
                    ms.inc(inst, sQH[hd % 2], 16)
                    qsw = rtsl(hd)
                    ms.inc(s.dma_start(out=qsw[0:64, :],
                                       in_=q_sp.ap()[hd * P + 64:hd * P + 128, :]),
                           sQH[hd % 2], 16)
                    inst = s.dma_start(out=qsw[64:128, :],
                                       in_=q_sp.ap()[hd * P:hd * P + 64, :])
                    ms.inc(inst, sQH[hd % 2], 16, name=f"qh_{hd}")
                    if KDBG:
                        ms.wait(s, sV, f"qp_{hd}")
                        inst = s.dma_start(
                            out=d_qp[hd * P:(hd + 1) * P, :],
                            in_=qevsl(hd))
                        ms.inc(inst, sCONST, 16, name=f"dbq_{hd}")
                if KDBG:
                    ms.wait(s, sV, "krope")
                    ms.inc(s.dma_start(out=d_k, in_=kfm[:, 0:8192]), sCONST, 16)
                    ms.inc(s.dma_start(out=d_v, in_=vtm[:, 0:8192]), sCONST, 16)
                    ms.inc(s.dma_start(out=d_q, in_=q_sp.ap()), sCONST, 16)
                    ms.wait(s, sA, "ex_0_15")
                    ms.inc(s.dma_start(out=d_eb, in_=EB[:, 0:8192]), sCONST, 16)

            @blk.vector
            def _(v):
                def qprep(hd):
                    if hd < 0 or hd >= 16:
                        return
                    ms.wait(v, sQH[hd % 2], f"qh_{hd}")
                    sl = qevsl(hd)
                    qsw = rtsl(hd)
                    v.tensor_tensor(H2[:, 0:1024], sl, cosq, ALU.mult)
                    v.tensor_tensor(SQB[:, 0:1024], qsw, sinq, ALU.mult)
                    v.tensor_tensor(SQB[:, 1024:2048], H2[:, 0:1024], SQB[:, 0:1024],
                                    ALU.add)
                    inst = v.tensor_tensor(sl, SQB[:, 1024:2048], QBC[:], ALU.mult)
                    ms.inc(inst, sV, 1, name=f"qp_{hd}")

                qprep(0)
                qprep(1)
                for it in range(32):
                    hd, tt = it // 2, it % 2
                    if tt == 1:
                        qprep(hd + 2)
                    if it >= 1:
                        pit = it - 1
                        phd = pit // 2
                        ms.wait(v, sA, f"rb_{pit}")
                        dst = A3[:, phd * 1024 + (pit % 2) * 512:
                                 phd * 1024 + (pit % 2) * 512 + 512]
                        inst = v.tensor_tensor(
                            dst, pE[pit % 2][:],
                            RBS[:, (pit % 2) * 512:(pit % 2) * 512 + 512], ALU.mult)
                        ms.inc(inst, sV, 1, name=f"encv_{pit}")
                    pump_v(v, pDEC, it * 2)
                    pump_v(v, pDEC, it * 2 + 1)
                pit = 31
                ms.wait(v, sA, f"rb_{pit}")
                dst = A3[:, 15 * 1024 + 512:15 * 1024 + 1024]
                inst = v.tensor_tensor(dst, pE[pit % 2][:],
                                       RBS[:, (pit % 2) * 512:(pit % 2) * 512 + 512],
                                       ALU.mult)
                ms.inc(inst, sV, 1, name=f"encv_{pit}")

            @blk.scalar
            def _(a):
                def wcv_load(i):
                    if i < 0 or i >= 32:
                        return
                    ch, h = i // 2, i % 2
                    inst = a.dma_start(
                        out=XT[:, (i % 2) * 1024:(i % 2) * 1024 + 1024],
                        in_=wo[ch * P:(ch + 1) * P, h * 1024:(h + 1) * 1024])
                    ms.inc(inst, sO1[0], 16, name=f"wcl_{i}")

                wcv_load(0)
                wcv_load(1)
                for it in range(32):
                    for stt in range(16):
                        ms.wait(a, sT, f"lg_{it}_{stt}")
                        inst = a.activation(EB[:, stt * 512:stt * 512 + 512],
                                            pL[stt % 2][:], AF.Exp)
                        ms.inc(inst, sA, 1, name=f"ex_{it}_{stt}")
                    ms.wait(a, sT, f"ss_{it}")
                    asyncf(a, a.activation(ROWS[0:1, 512:1024], pS[0:1, :], AF.Ln))
                    inst = a.activation(ROWS[0:1, 0:512], ROWS[0:1, 512:1024],
                                        AF.Exp, scale=-1.0)
                    ms.inc(inst, sA, 1, name=f"rr_{it}")
                    ms.wait(a, sT, f"pb_{it}")
                    inst = a.activation(RBS[:, (it % 2) * 512:(it % 2) * 512 + 512],
                                        pB[:], AF.Copy)
                    ms.inc(inst, sA, 1, name=f"rb_{it}")
                    ms.wait(a, sO1[0], f"wcl_{it}")
                    if it >= 4:
                        ms.wait(a, sO1[1], f"wcs_{it-4}")
                    inst = a.activation(
                        WQB[:, (it % 4) * 1024:(it % 4) * 1024 + 1024],
                        XT[:, (it % 2) * 1024:(it % 2) * 1024 + 1024], AF.Copy)
                    ms.inc(inst, sA, 1, name=f"wcc_{it}")
                    wcv_load(it + 2)

            @blk.tensor
            def _(t):
                for it in range(32):
                    hd, tt = it // 2, it % 2
                    kh = hd // 4
                    if tt == 0:
                        ms.wait(t, sV, f"qp_{hd}")
                    if it >= 2:
                        ms.wait(t, sV, f"encv_{it-2}")
                    qsl = qevsl(hd)[:, tt * 512:tt * 512 + 512]
                    for stt in range(16):
                        if stt >= 2:
                            ms.wait(t, sA, f"ex_{it}_{stt-2}")
                        elif it >= 1:
                            ms.wait(t, sA, f"ex_{it-1}_{14 + stt}")
                        lastm = t.matmul(
                            pL[stt % 2][:],
                            kfm[:, kh * 2048 + stt * P:kh * 2048 + stt * P + P],
                            qsl, start=True, stop=True)
                        ms.inc(lastm, sT, 1, name=f"lg_{it}_{stt}")
                        if stt >= 1:
                            ms.wait(t, sA, f"ex_{it}_{stt-1}")
                            eb = EB[:, (stt - 1) * 512:stt * 512]
                            t.matmul(pS[0:1, :], s_oc[:, 0:1], eb,
                                     start=(stt == 1), stop=False)
                            t.matmul(pE[it % 2][:],
                                     vtm[:, (stt - 1) * 512 + kh * P:
                                         (stt - 1) * 512 + kh * P + P],
                                     eb, start=(stt == 1), stop=False)
                    ms.wait(t, sA, f"ex_{it}_15")
                    eb = EB[:, 15 * 512:16 * 512]
                    lastm = t.matmul(pS[0:1, :], s_oc[:, 0:1], eb,
                                     start=False, stop=True)
                    ms.inc(lastm, sT, 1, name=f"ss_{it}")
                    lastm = t.matmul(pE[it % 2][:],
                                     vtm[:, 15 * 512 + kh * P:15 * 512 + kh * P + P],
                                     eb, start=False, stop=True)
                    ms.inc(lastm, sT, 1, name=f"pv_{it}")
                    ms.wait(t, sA, f"rr_{it}")
                    if it >= 1:
                        ms.wait(t, sA, f"rb_{it-1}")
                    lastm = t.matmul(pB[:], s_orf[0:1, :], ROWS[0:1, 0:512],
                                     start=True, stop=True)
                    ms.inc(lastm, sT, 1, name=f"pb_{it}")
                    pump_t(t, pDEC, it * 2)
                    pump_t(t, pDEC, it * 2 + 1)

            @blk.gpsimd
            def _(g):
                def wcv_store(i):
                    if i < 0 or i >= 32:
                        return
                    ch, h = i // 2, i % 2
                    ms.wait(g, sA, f"wcc_{i}")
                    dstv = wo_bf.ap().rearrange("p (q x) -> p q x", q=4)[
                        :, 2 * h:2 * h + 2, ch * 512:(ch + 1) * 512]
                    inst = g.dma_start(
                        out=dstv,
                        in_=WQB[:, (i % 4) * 1024:(i % 4) * 1024 + 1024
                                ].rearrange("p (a b) -> p a b", b=512))
                    ms.inc(inst, sO1[1], 16, name=f"wcs_{i}")

                for it in range(32):
                    pump_g(g, it * 2)
                    pump_g(g, it * 2 + 1)
                    wcv_store(it - 4)
                for i in range(28, 32):
                    wcv_store(i)
                ms.mark(sO1[1], "wocvD")

        ps2.close()
        if KLV < 3:
            return nc

        ps3 = contextlib.ExitStack()
        pW = [ps3.enter_context(nc.psum_tensor(f"pW{i}", [P, 512], F32))
              for i in range(4)]
        pT3 = [ps3.enter_context(nc.psum_tensor(f"pT3{i}", [P, 1024], BF16))
               for i in range(2)]

        def wsl3(i):
            return A5 if i % 2 == 0 else A6


        def ht3(t, tt8):
            if tt8 < 0 or tt8 > 7:
                return
            ms.wait(t, sV, f"lh2_{tt8}")
            if tt8 >= 1:
                ms.wait(t, sV, f"he_{tt8-1}")
            last = None
            for gi in range(2):
                for j in range(8):
                    cch = gi * 8 + j
                    last = t.transpose(pT3[gi][:, j * P:(j + 1) * P],
                                       H2[:, cch * P:(cch + 1) * P], s_id[:])
            ms.inc(last, sT, 1, name=f"ht_{tt8}")

        def he3(v, tt8):
            if tt8 < 0 or tt8 > 7:
                return
            ms.wait(v, sT, f"ht_{tt8}")
            dst = A2[:].rearrange("p (a b) -> p a b", b=1024)
            last = None
            for gi in range(2):
                srcp = pT3[gi][:].rearrange("p (a b) -> p a b", b=P)
                d3 = dst[:, gi * 8:gi * 8 + 8, tt8 * P:tt8 * P + P]
                last = v.tensor_copy(d3, srcp)
            ms.inc(last, sV, 1, name=f"he_{tt8}")

        with nc.Block() as blk:
            @blk.sync
            def _(s):
                for jp in range(4):
                    for t2 in range(2):
                        tt8 = 2 * jp + t2
                        if jp == 0 and t2 == 0:
                            ms.wait(s, sO1[1], "wocvD")
                        if jp >= 1:
                            ms.wait(s, sV, f"lh2_{tt8-2}")
                        inst = s.dma_start(
                            out=XT[:, t2 * 2048:(t2 + 1) * 2048],
                            in_=xo[tt8 * P:(tt8 + 1) * P, :])
                        ms.inc(inst, sXC, 16, name=f"xl_{tt8}")
                    for q in range(4):
                        i = jp * 4 + q
                        if i >= 2:
                            ms.wait(s, sT, f"wom_{i-2}_1")
                        inst = s.dma_start(
                            out=wsl3(i)[:],
                            in_=wo_bf.ap()[:, q * 8192:(q + 1) * 8192])
                        ms.inc(inst, sWO[i % 2], 16, name=f"wol_{i}")
                    for t2 in range(2):
                        tt8 = 2 * jp + t2
                        ms.wait(s, sV, f"oev_{tt8}_3")
                        inst = s.dma_start(out=out[tt8 * P:(tt8 + 1) * P, :],
                                           in_=XT[:, t2 * 2048:(t2 + 1) * 2048])
                        ms.inc(inst, sXR[t2], 16, name=f"xw_{tt8}")
                ms.mark(sXR[0], "xwD0")
                ms.mark(sXR[1], "xwD1")
                if KDBG:
                    ms.wait(s, sAG, "ag_g")
                    ms.inc(s.dma_start(out=d_wg, in_=ago["g"].ap()), sCONST, 16)
                    ms.wait(s, sV, "encv_31")
                    ms.inc(s.dma_start(out=d_enc, in_=A3[:, 0:16384]), sCONST, 16)
                    ms.wait(s, sV, "he_7")
                    ms.inc(s.dma_start(out=d_h, in_=A2[:, 0:16384]), sCONST, 16)

            @blk.tensor
            def _(t):
                for jp in range(4):
                    for q in range(4):
                        i = jp * 4 + q
                        if q == 0:
                            ht3(t, 2 * jp - 2)
                        if q == 2:
                            ht3(t, 2 * jp - 1)
                        ms.wait(t, sWO[i % 2], f"wol_{i}")
                        for t2 in range(2):
                            tt8 = 2 * jp + t2
                            k = i * 2 + t2
                            pb = pW[k % 4]
                            if k >= 4:
                                ip = i - 2
                                ms.wait(t, sV,
                                        f"oev_{2 * (ip // 4) + t2}_{ip % 4}")
                            for ch in range(16):
                                t.matmul(pb[:, 0:512],
                                         A3[:, ch * 1024 + tt8 * P:
                                            ch * 1024 + tt8 * P + P],
                                         wsl3(i)[:, ch * 512:(ch + 1) * 512],
                                         start=(ch == 0), stop=False)
                            ms.wait(t, sXC, f"xl_{tt8}")
                            if i == 0 and t2 == 0:
                                ms.wait(t, sA, "idf3")
                            lastm = t.matmul(pb[:, 0:512], RBS[:, 0:128].bitcast(F32),
                                             XT[:, t2 * 2048 + q * 512:
                                                t2 * 2048 + q * 512 + 512],
                                             start=False, stop=True)
                            ms.inc(lastm, sT, 1, name=f"wom_{i}_{t2}")
                        pump_t(t, pDEC, 64 + i * 2)
                        pump_t(t, pDEC, 64 + i * 2 + 1)
                        if i == 15:
                            pump_t(t, pDEC, 96)
                            pump_t(t, pDEC, 97)
                ht3(t, 6)
                ht3(t, 7)

            @blk.vector
            def _(v):
                for jp in range(4):
                    for q in range(4):
                        i = jp * 4 + q
                        for t2 in range(2):
                            tt8 = 2 * jp + t2
                            k = i * 2 + t2
                            ms.wait(v, sT, f"wom_{i}_{t2}")
                            inst = v.tensor_copy(
                                XT[:, t2 * 2048 + q * 512:
                                   t2 * 2048 + q * 512 + 512],
                                pW[k % 4][:, 0:512])
                            ms.inc(inst, sV, 1, name=f"oev_{tt8}_{q}")
                        pump_v(v, pDEC, 64 + i * 2)
                        pump_v(v, pDEC, 64 + i * 2 + 1)
                        if i == 15:
                            pump_v(v, pDEC, 96)
                            pump_v(v, pDEC, 97)
                        if q == 0:
                            he3(v, 2 * jp - 2)
                        if q == 2:
                            he3(v, 2 * jp - 1)
                    for t2 in range(2):
                        tt8 = 2 * jp + t2
                        xr = XT[:, t2 * 2048:(t2 + 1) * 2048]
                        ln_stats_v(v, xr, f"r{tt8}")
                        ms.wait(v, sA, f"lr_r{tt8}")
                        ms.wait(v, sT, f"ht_{tt8-1}")
                        inst = v.tensor_scalar(H2[:], xr, STA[:, 1:2],
                                               STA[:, 5:6],
                                               op0=ALU.subtract, op1=ALU.mult)
                        ms.inc(inst, sV, 1, name=f"lh2_{tt8}")
                he3(v, 6)
                he3(v, 7)

            @blk.scalar
            def _(a):
                inst = a.activation(RBS[:, 0:128].bitcast(F32), s_id[:], AF.Copy)
                ms.inc(inst, sA, 1, name="idf3")
                for tt8 in range(8):
                    xr = XT[:, (tt8 % 2) * 2048:(tt8 % 2) * 2048 + 2048]
                    ln_stats_a(a, xr, f"r{tt8}")

            @blk.gpsimd
            def _(g):
                for i in range(16):
                    pump_g(g, 64 + i * 2)
                    pump_g(g, 64 + i * 2 + 1)
                    if i == 2:
                        ag_trig(g, "u")
                    if i == 10:
                        ag_trig(g, "d1")
                    if i == 15:
                        pump_g(g, 96)
                        pump_g(g, 97)

        ps3.close()
        if KLV < 4:
            return nc

        with nc.Block() as blk:
            @blk.vector
            def _(v):
                for p in range(98, NPUMP):
                    pump_v(v, pDEC, p)

            @blk.tensor
            def _(t):
                for p in range(98, NPUMP):
                    pump_t(t, pDEC, p)

            @blk.gpsimd
            def _(g):
                for p in range(98, NPUMP):
                    pump_g(g, p)
                ag_trig(g, "d2")

        psD.close()
        if KLV < 5:
            return nc

        ps5 = contextlib.ExitStack()
        pG = ps5.enter_context(nc.psum_tensor("pG", [P, 1024], F32))
        pU = ps5.enter_context(nc.psum_tensor("pU", [P, 1024], F32))
        pDn = [ps5.enter_context(nc.psum_tensor(f"pDn{i}", [P, 512], F32))
               for i in range(2)]

        def isl(gc):
            if gc < 8:
                return A5[:, gc * 1024:gc * 1024 + 1024]
            return A6[:, (gc - 8) * 1024:(gc - 8) * 1024 + 1024]

        def accL(b, cc, tt8):
            return b * 32 + cc * 8 + tt8

        def accN(L):
            b, r = divmod(L, 32)
            cc, tt8 = divmod(r, 8)
            return f"acc_{b}_{cc}_{tt8}"

        with nc.Block() as blk:
            if KDBG:
                @blk.sync
                def _(s):
                    ms.wait(s, sAG, "ag_d1")
                    ms.wait(s, sAG, "ag_d2")
                    ms.inc(s.dma_start(out=d_wd[:, 0:1024], in_=ago_dh[0].ap()),
                           sCONST, 16)
                    ms.inc(s.dma_start(out=d_wd[:, 1024:2048], in_=ago_dh[1].ap()),
                           sCONST, 16)

            @blk.sync
            def _(s):
                for g in range(128):
                    if g == 0:
                        ms.wait(s, sAG, "ag_g")
                        ms.wait(s, sAG, "ag_u")
                        ms.wait(s, sV, "dv_95")
                        ms.wait(s, sT, "pv_31")
                    if g >= 2:
                        ms.wait(s, sT, f"um_{g - 2}")
                    rank, rtl = g // 16, g % 16
                    for m, arr, base in (("g", ago["g"], 0), ("u", ago["u"], 4096)):
                        srcw = arr.ap().rearrange("a (p f) -> a p f", p=P)[
                            rank * 16 + rtl, :, :]
                        dst = A4[:, base + (g % 2) * 2048:base + (g % 2) * 2048 + 2048]
                        inst = s.dma_start(out=dst, in_=srcw)
                        ms.inc(inst, sWG[g % 2], 16, name=f"w{m}l_{g}")

            @blk.scalar
            def _(a):
                for b in range(8):
                    for gc in range(16):
                        g = b * 16 + gc
                        ms.wait(a, sT, f"gm_{g}")
                        if g >= 2:
                            g2 = g - 2
                            ms.wait(a, sV, f"int_{g2 // 16}_{g2 % 16}")
                        inst = a.activation(A7[:, (g % 2) * 1024:(g % 2) * 1024 + 1024],
                                            pG[:], AF.Gelu_apprx_tanh)
                        ms.inc(inst, sA, 1, name=f"gg_{g}")
                        ms.wait(a, sT, f"um_{g}")
                        inst = a.activation(A7[:, 2048 + (g % 2) * 1024:
                                               2048 + (g % 2) * 1024 + 1024],
                                            pU[:], AF.Copy)
                        ms.inc(inst, sA, 1, name=f"gu_{g}")

            @blk.tensor
            def _(t):
                for b in range(8):
                    for gc in range(16):
                        g = b * 16 + gc
                        ms.wait(t, sWG[g % 2], f"wgl_{g}")
                        ms.wait(t, sWG[g % 2], f"wul_{g}")
                        if g == 0:
                            ms.wait(t, sV, "he_7")
                        if g >= 1:
                            ms.wait(t, sA, f"gg_{g-1}")
                            ms.wait(t, sA, f"gu_{g-1}")
                        wgs = A4[:, (g % 2) * 2048:(g % 2) * 2048 + 2048]
                        wus = A4[:, 4096 + (g % 2) * 2048:4096 + (g % 2) * 2048 + 2048]
                        lastm = None
                        for cch in range(16):
                            for tt2 in range(2):
                                lastm = t.matmul(
                                    pG[:, tt2 * 512:tt2 * 512 + 512],
                                    wgs[:, cch * 128:cch * 128 + 128],
                                    A2[:, cch * 1024 + tt2 * 512:
                                       cch * 1024 + tt2 * 512 + 512],
                                    start=(cch == 0), stop=(cch == 15))
                        ms.inc(lastm, sT, 1, name=f"gm_{g}")
                        lastm = None
                        for cch in range(16):
                            for tt2 in range(2):
                                lastm = t.matmul(
                                    pU[:, tt2 * 512:tt2 * 512 + 512],
                                    wus[:, cch * 128:cch * 128 + 128],
                                    A2[:, cch * 1024 + tt2 * 512:
                                       cch * 1024 + tt2 * 512 + 512],
                                    start=(cch == 0), stop=(cch == 15))
                        ms.inc(lastm, sT, 1, name=f"um_{g}")
                    ms.wait(t, sV, f"int_{b}_15")
                    for cc in range(4):
                        ms.wait(t, sWD[(b * 4 + cc) % 2], f"wdl_{b}_{cc}")
                        for tt8 in range(8):
                            i = cc * 8 + tt8
                            if i >= 2:
                                pi = i - 2
                                ms.wait(t, sV, f"o2e_{b}_{pi // 8}_{pi % 8}")
                            lastm = None
                            for gc in range(16):
                                lastm = t.matmul(
                                    pDn[i % 2][:],
                                    isl(gc)[:, tt8 * P:tt8 * P + P],
                                    A3[:, ((b * 4 + cc) % 2) * 8192 + gc * 512:
                                       ((b * 4 + cc) % 2) * 8192 + gc * 512 + 512],
                                    start=(gc == 0), stop=(gc == 15))
                            ms.inc(lastm, sT, 1, name=f"dn_{b}_{cc}_{tt8}")

            @blk.vector
            def _(v):
                for b in range(8):
                    for gc in range(16):
                        g = b * 16 + gc
                        ms.wait(v, sA, f"gg_{g}")
                        ms.wait(v, sA, f"gu_{g}")
                        if b >= 1 and gc == 0:
                            ms.wait(v, sT, f"dn_{b-1}_3_7")
                        inst = v.tensor_tensor(
                            isl(gc), A7[:, (g % 2) * 1024:(g % 2) * 1024 + 1024],
                            A7[:, 2048 + (g % 2) * 1024:2048 + (g % 2) * 1024 + 1024],
                            ALU.mult)
                        ms.inc(inst, sV, 1, name=f"int_{b}_{gc}")
                    for cc in range(4):
                        for tt8 in range(8):
                            i = cc * 8 + tt8
                            L = accL(b, cc, tt8)
                            ms.wait(v, sT, f"dn_{b}_{cc}_{tt8}")
                            if L >= 2:
                                ms.wait(v, sAC[L % 2], accN(L - 2))
                            o2 = A7[:, 4096 + (i % 2) * 1024:4096 + (i % 2) * 1024 + 1024]
                            inst = v.tensor_copy(o2.bitcast(F32), pDn[i % 2][:])
                            ms.inc(inst, sV, 1, name=f"o2e_{b}_{cc}_{tt8}")

            @blk.gpsimd
            def _(g):
                def wds_load(g, b, cc):
                    if b > 7:
                        return
                    idx = b * 4 + cc
                    if idx == 0:
                        ms.wait(g, sAG, "ag_d1")
                        ms.wait(g, sT, "wom_15_1")
                    if idx == 2:
                        ms.wait(g, sAG, "ag_d2")
                    if idx >= 2:
                        pcc = idx - 2
                        ms.wait(g, sT, f"dn_{pcc // 4}_{pcc % 4}_7")
                    lastd = None
                    for gc2 in range(16):
                        srcd = ago_dh[cc // 2].ap()[
                            (b * 16 + gc2) * P:(b * 16 + gc2 + 1) * P,
                            (cc % 2) * 512:(cc % 2) * 512 + 512]
                        dst = A3[:, (idx % 2) * 8192 + gc2 * 512:
                                 (idx % 2) * 8192 + gc2 * 512 + 512]
                        lastd = g.dma_start(out=dst, in_=srcd)
                        ms.inc(lastd, sWD[idx % 2], 16)
                    ms.mark(sWD[idx % 2], f"wdl_{b}_{cc}")

                ms.wait(g, sXR[0], "xwD0")
                ms.wait(g, sXR[1], "xwD1")
                wds_load(g, 0, 0)
                wds_load(g, 0, 1)
                for b in range(8):
                    for cc in range(4):
                        for tt8 in range(8):
                            i = cc * 8 + tt8
                            L = accL(b, cc, tt8)
                            ms.wait(g, sV, f"o2e_{b}_{cc}_{tt8}")
                            if b >= 1:
                                ms.wait(g, sAC[L % 2], f"acc_{b-1}_{cc}_{tt8}")
                            o2 = A7[:, 4096 + (i % 2) * 1024:
                                    4096 + (i % 2) * 1024 + 1024]
                            inst = g.dma_start(
                                out=out[tt8 * P:(tt8 + 1) * P, cc * 512:(cc + 1) * 512],
                                in_=o2.bitcast(F32), accum_op=ALU.add)
                            ms.inc(inst, sAC[L % 2], 16, name=f"acc_{b}_{cc}_{tt8}")
                        if cc == 0:
                            wds_load(g, b, 2)
                        elif cc == 1:
                            wds_load(g, b, 3)
                        elif cc == 2:
                            wds_load(g, b + 1, 0)
                        else:
                            wds_load(g, b + 1, 1)
                ms.wait(g, sAC[0], "acc_7_3_6")
                ms.wait(g, sAC[1], "acc_7_3_7")

        ps5.close()
    return nc


_CACHE = {}


def _consts():
    if "c" in _CACHE:
        return _CACHE["c"]
    HH = 128
    fraction = (np.arange(0, HH, 2) / HH).astype(np.float32)
    timescale = (10000.0 ** fraction).astype(np.float32)
    sinus = np.arange(TA, dtype=np.float32)[:, None] * (1.0 / timescale)[None, :]
    sinus = np.concatenate([sinus, sinus], axis=1)
    sin_t = np.sin(sinus).astype(np.float32)
    cos_t = np.cos(sinus).astype(np.float32)
    s = np.float32(1.0 / np.sqrt(HH))
    bf = ml_dtypes.bfloat16
    cosq = (cos_t.T * s).astype(bf)
    sinq = (np.concatenate([-sin_t.T[:64], sin_t.T[64:]], axis=0) * s).astype(bf)
    cosk = cos_t.T.astype(bf)
    sink = np.concatenate([-sin_t.T[:64], sin_t.T[64:]], axis=0).astype(bf)
    ident = np.eye(P).astype(bf)
    onesc = np.ones((P, 1), dtype=bf)
    onesrf = np.ones((1, P), dtype=np.float32)
    _CACHE["c"] = (cosq, sinq, cosk, sink, ident, onesc, onesrf)
    return _CACHE["c"]


def _program():
    if "nc" in _CACHE:
        return _CACHE["nc"]
    ms1 = MS()
    build(ms1)
    ms2 = MS(prev=ms1.out, semof_prev=ms1.semof)
    nc = build(ms2)
    assert ms1.cnt == ms2.cnt, "pass mismatch"
    _CACHE["nc"] = nc
    return nc


def kernel(L0_attn_x, L0_attn_Wq, L0_attn_Wk, L0_attn_Wv, L0_attn_Wout,
           L0_ffn_gate_W_packed, L0_ffn_gate_scales, L0_ffn_up_W_packed,
           L0_ffn_up_scales, L0_ffn_down_W_packed, L0_ffn_down_scales):
    x = np.ascontiguousarray(np.asarray(L0_attn_x, dtype=np.float32))
    wqh = np.ascontiguousarray(np.asarray(L0_attn_Wq, dtype=np.float32))
    wkh = np.ascontiguousarray(np.asarray(L0_attn_Wk, dtype=np.float32))
    wvh = np.ascontiguousarray(np.asarray(L0_attn_Wv, dtype=np.float32))
    woh = np.ascontiguousarray(np.asarray(L0_attn_Wout, dtype=np.float32))
    gpk = np.ascontiguousarray(np.asarray(L0_ffn_gate_W_packed, dtype=np.int32))
    gsc = np.ascontiguousarray(np.asarray(L0_ffn_gate_scales, dtype=np.float32))
    upk = np.ascontiguousarray(np.asarray(L0_ffn_up_W_packed, dtype=np.int32))
    usc = np.ascontiguousarray(np.asarray(L0_ffn_up_scales, dtype=np.float32))
    dpk = np.ascontiguousarray(np.asarray(L0_ffn_down_W_packed, dtype=np.int32))
    dsc = np.ascontiguousarray(np.asarray(L0_ffn_down_scales, dtype=np.float32))
    cosq, sinq, cosk, sink, ident, onesc, onesrf = _consts()
    epsv = np.full((P, 1), 1e-6, dtype=np.float32)

    nc = _program()
    in_maps = []
    for c in range(8):
        b, th = c // 2, c % 2
        xown = np.ascontiguousarray(x[b, th * TO:(th + 1) * TO])
        xoth = np.ascontiguousarray(x[b, (1 - th) * TO:(2 - th) * TO])
        tq = np.ascontiguousarray(np.concatenate(
            [cosq[:, th * TO:(th + 1) * TO], sinq[:, th * TO:(th + 1) * TO]], axis=1))
        ckp = np.concatenate([cosk[:, th * TO:(th + 1) * TO],
                              cosk[:, (1 - th) * TO:(2 - th) * TO]], axis=1)
        skp = np.concatenate([sink[:, th * TO:(th + 1) * TO],
                              sink[:, (1 - th) * TO:(2 - th) * TO]], axis=1)
        tk = np.ascontiguousarray(np.concatenate([ckp, skp], axis=1))
        in_maps.append(dict(
            xo=xown, xz=xoth, wq=wqh, wk=wkh, wv=wvh, wo=woh,
            gpk=np.ascontiguousarray(gpk[c * NSL:(c + 1) * NSL]),
            gsc=np.ascontiguousarray(gsc[c * NSL:(c + 1) * NSL]),
            upk=np.ascontiguousarray(upk[c * NSL:(c + 1) * NSL]),
            usc=np.ascontiguousarray(usc[c * NSL:(c + 1) * NSL]),
            dpk=np.ascontiguousarray(dpk[:, c * 1024:(c + 1) * 1024]),
            dsc=np.ascontiguousarray(dsc[:, c * 64:(c + 1) * 64]),
            identd=ident, onescd=onesc, onesrfd=onesrf, epsd=epsv,
            tblq=tq, tblk=tk))
    res = run_bass_kernel_spmd(nc, in_maps, core_ids=list(range(8)))
    outv = np.empty((4, TA, C), dtype=np.float32)
    for c in range(8):
        b, th = c // 2, c % 2
        outv[b, th * TO:(th + 1) * TO] = res.results[c]["out"]
    return outv
